# version 08
# baseline (speedup 1.0000x reference)
import numpy as np
import ml_dtypes
from contextlib import ExitStack

import concourse.bass as bass
import concourse.mybir as mybir
from concourse.bass_utils import run_bass_kernel_spmd

F32 = mybir.dt.float32
BF16 = mybir.dt.bfloat16
I32 = mybir.dt.int32
AF = mybir.ActivationFunctionType
ALU = mybir.AluOpType
AX = mybir.AxisListType
BF = ml_dtypes.bfloat16

NCORE = 8
D = 1024
NF = 8
SEQ = 8192
B = 2
CF = 4096
TB = 512
NTB = CF // TB
NKT = SEQ // 128
MMAX = 4096

ENGS = ("sync", "tensor", "vector", "scalar", "gpsimd")
SKIP = set()
_uid = [0]


def uid():
    _uid[0] += 1
    return _uid[0]


def q0(tb, role):
    return (2 * tb + role) * TB


class Ctr:
    def __init__(self, sem):
        self.sem = sem
        self.v = 0
        self.peak = 0

    def at(self):
        return (self.sem, self.v)


class Prog:
    def __init__(self):
        self.nc = bass.Bass("TRN2", target_bir_lowering=False)
        self.prog_stack = ExitStack()
        self.phase_stack = ExitStack()
        self.ops = {e: [] for e in ENGS}
        self.pending = {e: [] for e in ENGS}
        self.ctrs = {}
        self.sem_list = []
        self.reg = {}
        self.m0 = [0 * 256, 0 * 256] + [0] * 6 + list(range(1, 25))
        self.m1 = [1 * 256, 1 * 256] + [0] * 6 + list(range(1, 25))
        self.bar = self._sem("bar")
        self.bar_n = 0

    def _sem(self, name):
        h = self.prog_stack.enter_context(self.nc.semaphore(f"{name}_{uid()}"))
        self.sem_list.append(h)
        return h

    def ctr(self, name):
        if name not in self.ctrs:
            self.ctrs[name] = Ctr(self._sem(name))
        return self.ctrs[name]

    def sbuf(self, name, shape, dt=F32, prog=False):
        st = self.prog_stack if prog else self.phase_stack
        return st.enter_context(
            self.nc.sbuf_tensor(f"{name}_{uid()}", shape, dt))

    def psum(self, name, shape, dt=F32):
        return self.phase_stack.enter_context(
            self.nc.psum_tensor(f"{name}_{uid()}", shape, dt))

    def midx(self, v0, v1):
        self.m0.append(int(v0))
        self.m1.append(int(v1))
        return len(self.m0) - 1

    def on(self, eng, fn, waits=(), inc=None, inc_k=1):
        ws = list(self.pending[eng])
        self.pending[eng] = []
        if isinstance(waits, Ctr):
            waits = (waits,)
        for w in waits:
            ws.append(w.at() if isinstance(w, Ctr) else (w[0], w[1]))
        self.ops[eng].append((fn, ws, inc.sem if inc else None, inc_k))
        if inc is not None:
            inc.v += inc_k
            inc.peak = max(inc.peak, inc.v)

    def dma(self, eng, dst_fn, src_fn, ctr, waits=()):
        self.on(eng, lambda e: e.dma_start(dst_fn(), src_fn()),
                waits=waits, inc=ctr, inc_k=16)

    def dma_wreg(self, eng, tensor, pattern, off0, off1, src_fn, ctr,
                 waits=()):
        idx = self.midx(off0, off1)
        self.on(eng, lambda e, idx=idx: e.reg_load(
            self.reg[eng], self.mt_s[0:1, idx:idx + 1]),
            waits=tuple(waits) + (self.mld,))
        self.on(eng, lambda e, tensor=tensor, pattern=pattern:
                e.dma_start(bass.AP(tensor, self.reg[eng], pattern),
                            src_fn()),
                inc=ctr, inc_k=16)

    def dma_rreg(self, eng, dst_fn, tensor, pattern, off0, off1, ctr,
                 waits=()):
        idx = self.midx(off0, off1)
        self.on(eng, lambda e, idx=idx: e.reg_load(
            self.reg[eng], self.mt_s[0:1, idx:idx + 1]),
            waits=tuple(waits) + (self.mld,))
        self.on(eng, lambda e, tensor=tensor, pattern=pattern:
                e.dma_start(dst_fn(),
                            bass.AP(tensor, self.reg[eng], pattern)),
                inc=ctr, inc_k=16)

    def barrier(self):
        snaps = [(c.sem, c.v) for c in self.ctrs.values() if c.v > 0]
        self.bar_n += 1
        for eng in ("tensor", "vector", "scalar", "sync"):
            self.on(eng, lambda e: e.sem_inc(self.bar, 1))
        self.on("gpsimd", lambda e: e.sem_inc(self.bar, 1), waits=snaps)
        t = 5 * self.bar_n
        for eng in ENGS:
            self.pending[eng].append((self.bar, t))
        self.phase_stack.close()
        self.phase_stack = ExitStack()

    def pair_sync(self, k, npoll=512):
        sd = self._sem(f"sd{k}")
        po = self.ctr(f"poS{k}")
        stf = self.ctr("stf")
        self.on("scalar", lambda e: e.reg_load(
            self.reg["scalar"], self.mt_s[0:1, 0:1]),
            waits=(self.mld, (self.zd, 1)))
        self.on("scalar", lambda e, k=k: e.dma_start(
            bass.AP(self.fl, self.reg["scalar"], [[512, 1], [1, 4]]),
            self.mt_s[0:1, 8 + k - 1: 8 + k + 3]),
            inc=stf, inc_k=16)
        self.on("gpsimd", lambda e: e.nop(), waits=[(self.zd, 1)])
        for i in range(npoll):
            if i % 2 == 1:
                self.dma("gpsimd", lambda: self.dly_sb[:, :],
                         lambda: self.dly_src[0:128, 0:1024], po)
            self.dma("gpsimd", lambda: self.fv_t[0:1, 0:1],
                     lambda: self.fl[0:1, 0:1], po)
            self.dma("gpsimd", lambda: self.fv_t[0:1, 1:2],
                     lambda: self.fl[0:1, 256:257], po)
            v = po.v
            self.on("gpsimd", lambda e: e.reg_load(
                self.reg["g0"], self.fv_t[0:1, 0:1]),
                waits=[(po.sem, v)])
            self.on("gpsimd", lambda e: e.reg_load(
                self.reg["g1"], self.fv_t[0:1, 1:2]))
            self.on("gpsimd", lambda e: e.reg_alu(
                self.reg["g1"], self.reg["g1"], self.reg["g0"],
                ALU.min))

            def chk(e, k=k, sd=sd):
                with e.If_cmp(self.reg["g1"], k, "IS_GE"):
                    e.sem_inc(sd, 1)
            self.on("gpsimd", chk)
        self.on("gpsimd", lambda e: e.nop(), waits=[(sd, 1)])
        return sd

    def emit(self):
        nc = self.nc
        sems = list(self.sem_list)
        with nc.Block() as blk:
            @blk.gpsimd
            def _(g):
                for s in sems:
                    g.sem_clear(s)
        ops = self.ops
        with nc.Block() as blk:
            for eng in ENGS:
                if not ops[eng]:
                    continue

                bar_id = id(self.bar)

                def mk(name):
                    def runner(e):
                        maxw = {}
                        for fn, ws, sem, kk in ops[name]:
                            for (s, v) in ws:
                                key = id(s)
                                if v > 0 and maxw.get(key, -1) < v:
                                    e.wait_ge(s, v)
                                    if key == bar_id:
                                        maxw = {}
                                    maxw[key] = v
                            inst = fn(e)
                            if sem is not None:
                                assert inst is not None
                                inst.then_inc(sem, kk)

                    def body(e):
                        if name == "sync":
                            with e.register(f"r_sy_{uid()}") as r:
                                self.reg["sync"] = r
                                runner(e)
                        elif name == "scalar":
                            with e.register(f"r_sc_{uid()}") as r:
                                self.reg["scalar"] = r
                                runner(e)
                        elif name == "gpsimd":
                            with e.register(f"r_g_{uid()}") as r, \
                                    e.register(f"r_g0_{uid()}") as r0, \
                                    e.register(f"r_g1_{uid()}") as r1:
                                self.reg["gpsimd"] = r
                                self.reg["g0"] = r0
                                self.reg["g1"] = r1
                                runner(e)
                        else:
                            runner(e)
                    return body
                getattr(blk, eng)(mk(eng))
        self.phase_stack.close()
        self.prog_stack.close()
        return nc



def host_consts():
    cf = np.zeros((128, 8), np.float32)
    cf[:, 0] = 1.0
    cf[0:64, 1] = 1.0
    cf[64:128, 2] = 1.0
    cb = np.zeros((128, 132), np.float32)
    cb[:, 0:128] = np.eye(128, dtype=np.float32)
    cb[:, 128] = 1.0
    ct = np.zeros((2, 128), np.float32)
    ct[0, 0:64] = 1.0
    ct[1, 64:128] = 1.0
    onesr = np.ones((1, 128), np.float32)
    return {"c_f32": cf, "c_bf16": cb, "c_rows": ct, "c_ones": onesr}


def load_consts(p, ld):
    nc = p.nc
    cf_d = nc.dram_tensor("c_f32", [128, 8], F32, kind="ExternalInput")
    cb_d = nc.dram_tensor("c_bf16", [128, 132], F32, kind="ExternalInput")
    ct_d = nc.dram_tensor("c_rows", [2, 128], F32, kind="ExternalInput")
    co_d = nc.dram_tensor("c_ones", [1, 128], F32, kind="ExternalInput")
    cf = p.sbuf("cf", [128, 8], F32, prog=True)
    cb = p.sbuf("cb", [128, 132], BF16, prog=True)
    ct = p.sbuf("ct", [2, 128], F32, prog=True)
    co = p.sbuf("co", [1, 128], F32, prog=True)
    cob = p.sbuf("cob", [1, 128], BF16, prog=True)
    p.dma("sync", lambda: cf[:, :], lambda: cf_d[:, :], ld)
    p.dma("gpsimd", lambda: cb[:, :], lambda: cb_d[:, :], ld)
    p.dma("sync", lambda: ct[:, :], lambda: ct_d[:, :], ld)
    p.dma("sync", lambda: co[:, :], lambda: co_d[:, :], ld)
    p.dma("gpsimd", lambda: cob[:, :], lambda: co_d[:, :], ld)
    return {
        "ones_col": cf[:, 0:1],
        "sel64": cf[:, 1:3],
        "ident": cb[:, 0:128],
        "ones_col_bf": cb[:, 128:129],
        "sel64T": ct[0:2, :],
        "ones_row": co[0:1, :],
        "ones_row_bf": cob[0:1, :],
    }


def emit_zero_handshake(p, npoll=1024):
    zv = p.zv
    zinit = p.ctr("zinit")
    p.on("vector", lambda e: e.memset(zv[:, :], 0), inc=zinit)
    zh = p.ctr("zh")
    zd1 = p._sem("zd1")
    p.on("gpsimd", lambda e: e.reg_load(p.reg["gpsimd"], p.mt_s[0:1, 0:1]),
         waits=(p.mld, zinit))
    p.on("gpsimd", lambda e: e.dma_start(
        bass.AP(p.zl, p.reg["gpsimd"], [[512, 1], [1, 4]]),
        p.mt_s[0:1, 4:8]),
        inc=zh, inc_k=16)
    def poll_eq(sess_idx, first_wait, sem):
        po = p.ctr(f"poH{sess_idx}")
        for i in range(npoll):
            if i % 2 == 1:
                p.dma("gpsimd", lambda: p.dly_sb[:, :],
                      lambda: p.dly_src[0:128, 0:1024], po,
                      waits=first_wait)
            p.dma("gpsimd", lambda: p.fv_t[0:1, 2:3],
                  lambda: p.zl[0:1, 0:1], po, waits=first_wait)
            p.dma("gpsimd", lambda: p.fv_t[0:1, 3:4],
                  lambda: p.zl[0:1, 256:257], po)
            v = po.v
            p.on("gpsimd", lambda e: e.reg_load(
                p.reg["gpsimd"], p.mt_s[0:1, sess_idx:sess_idx + 1]),
                waits=[(po.sem, v)])
            p.on("gpsimd", lambda e: e.reg_load(p.reg["g0"],
                                                p.fv_t[0:1, 2:3]))
            p.on("gpsimd", lambda e: e.reg_alu(
                p.reg["g0"], p.reg["g0"], p.reg["gpsimd"], ALU.subtract))
            p.on("gpsimd", lambda e: e.reg_load(p.reg["g1"],
                                                p.fv_t[0:1, 3:4]))
            p.on("gpsimd", lambda e: e.reg_alu(
                p.reg["g1"], p.reg["g1"], p.reg["gpsimd"], ALU.subtract))
            p.on("gpsimd", lambda e: e.reg_alu(
                p.reg["g1"], p.reg["g1"], p.reg["g0"], ALU.bitwise_or))

            def chk(e):
                with e.If_cmp(p.reg["g1"], 0, "IS_EQ"):
                    e.sem_inc(sem, 1)
            p.on("gpsimd", chk)
        p.on("gpsimd", lambda e: e.nop(), waits=[(sem, 1)])

    poll_eq(4, [(zh.sem, 16)], zd1)
    p.on("gpsimd", lambda e: e.reg_load(p.reg["gpsimd"], p.mt_s[0:1, 0:1]))
    p.on("gpsimd", lambda e: e.dma_start(
        bass.AP(p.fl, p.reg["gpsimd"], [[512, 1], [1, 4]]), zv[0:1, 0:4]),
        inc=zh, inc_k=16)
    p.on("gpsimd", lambda e: e.dma_start(
        bass.AP(p.zl, p.reg["gpsimd"], [[512, 1], [1, 4]]),
        p.mt_s[0:1, 5:9]),
        waits=[(zh.sem, 32)], inc=zh, inc_k=16)
    poll_eq(5, [(zh.sem, 48)], p.zd)



def emit_wslabs(p, dram, nk, M, ctr, name="wsl", row0=0, waits=()):
    slabs = [p.sbuf(f"{name}{k}", [128, M], BF16) for k in range(nk)]
    for k in range(nk):
        p.on("gpsimd",
             lambda e, k=k: e.dma_start(
                 slabs[k][:, :],
                 dram[row0 + k * 128:row0 + (k + 1) * 128, :]),
             waits=waits, inc=ctr, inc_k=16)
    return slabs


def emit_mm_chain(p, lhsT_fn, rhs_fn, y_ps, nk, nm, ntb, mm, ev, waits_base,
                  evac_fn, wld=None, nbank=None):
    nbank = nbank if nbank is not None else len(y_ps)
    for m in range(nm):
        for tb in range(ntb):
            for k in range(nk):
                waits = list(waits_base)
                if wld is not None:
                    waits.append((wld.sem, 16 * (k + 1)))
                if k == 0:
                    waits.append(ev)
                lhsT = lhsT_fn(k, m)
                rhs = rhs_fn(k, tb)
                p.on("tensor",
                     lambda e, lhsT=lhsT, rhs=rhs, k=k, tb=tb: e.matmul(
                         y_ps[tb % nbank][:, :], lhsT, rhs,
                         start=(k == 0), stop=(k == nk - 1)),
                     waits=waits, inc=(mm if k == nk - 1 else None))
            evac_fn(m, tb)


def ln_stream(p, C, src_d, done, st, dst_sb=None, dst_d=None, sums_sb=None,
              pre=(), T=CF):
    ntb = T // TB
    pre = tuple(pre) + ((st.sem, st.v),)
    xw = [[p.sbuf(f"lnw{i}_{f}", [128, TB], F32) for f in range(NF)]
          for i in range(2)]
    sq = p.sbuf("ln_sq", [128, TB * NF], F32)
    red = p.sbuf("ln_red", [128, 1], F32)
    ps_stack = ExitStack()
    old_ps = p.phase_stack
    p.phase_stack = ps_stack
    stat = p.psum("ln_stat", [1, 2 * TB], F32)
    ab = p.psum("ln_ab", [128, 2 * TB], F32)
    p.phase_stack = old_ps
    mu = p.sbuf("ln_mu", [1, TB], F32)
    rr = p.sbuf("ln_rr", [1, TB], F32)
    musq = p.sbuf("ln_musq", [1, TB], F32)
    brow = p.sbuf("ln_brow", [1, TB], F32)
    sq_done = p.ctr("lnsq")
    st_done = p.ctr("lnst")
    row_done = p.ctr("lnrow")
    ab_done = p.ctr("lnab")
    wldn = p.ctr("lnwl")
    if sums_sb is not None:
        zero = p.ctr("lnz")
        p.on("vector", lambda e: e.memset(sums_sb[:, :], 0.0), inc=zero)
    msl = slice(0, TB)
    for tb in range(ntb):
        w = tb % 2
        for f in range(NF):
            p.dma("sync", lambda f=f, w=w: xw[w][f][:, :],
                  lambda f=f, tb=tb: src_d[f * 128:(f + 1) * 128,
                                           tb * TB:(tb + 1) * TB],
                  wldn, waits=tuple(pre) + (sq_done, st_done))
        wnow = (wldn.sem, wldn.v)
        for f in range(NF):
            fsl = slice(f * TB, (f + 1) * TB)
            p.on("scalar",
                 lambda e, f=f, w=w, fsl=fsl: e.activation(
                     sq[:, fsl], xw[w][f][:, :], AF.Square),
                 waits=(wnow, done, st), inc=sq_done)
        for f in range(NF):
            fsl = slice(f * TB, (f + 1) * TB)
            p.on("tensor",
                 lambda e, f=f, w=w: e.matmul(
                     stat[0:1, 0:TB], C["ones_col"], xw[w][f][:, :],
                     start=(f == 0), stop=(f == NF - 1)),
                 waits=(wnow, row_done),
                 inc=(st_done if f == NF - 1 else None))
            p.on("tensor",
                 lambda e, f=f, fsl=fsl: e.matmul(
                     stat[0:1, TB:2 * TB], C["ones_col"], sq[:, fsl],
                     start=(f == 0), stop=(f == NF - 1)),
                 waits=(sq_done, row_done),
                 inc=(st_done if f == NF - 1 else None))
        p.on("vector", lambda e: e.tensor_scalar_mul(
            mu[:, msl], stat[0:1, 0:TB], 1.0 / D),
            waits=(st_done, ab_done), inc=row_done)
        p.on("vector", lambda e: e.tensor_scalar_mul(
            rr[:, msl], stat[0:1, TB:2 * TB], 1.0 / D),
            waits=(st_done, ab_done), inc=row_done)
        p.on("vector", lambda e: e.tensor_tensor(
            musq[:, msl], mu[:, msl], mu[:, msl], ALU.mult),
            waits=(row_done,), inc=row_done)
        p.on("vector", lambda e: e.tensor_tensor(
            rr[:, msl], rr[:, msl], musq[:, msl], ALU.subtract),
            waits=(row_done,), inc=row_done)
        p.on("scalar", lambda e: e.activation(
            rr[:, msl], rr[:, msl], AF.Sqrt, bias=np.float32(1e-5)),
            waits=(row_done,), inc=row_done)
        p.on("vector", lambda e: e.reciprocal(rr[:, msl], rr[:, msl]),
             waits=(row_done,), inc=row_done)
        p.on("vector", lambda e: e.scalar_tensor_tensor(
            brow[:, msl], mu[:, msl], -1.0, rr[:, msl], ALU.mult, ALU.mult),
            waits=(row_done,), inc=row_done)
        p.on("tensor", lambda e: e.matmul(
            ab[:, 0:TB], C["ones_row"], rr[:, msl], start=True, stop=True),
            waits=(row_done, sq_done, done), inc=ab_done)
        p.on("tensor", lambda e: e.matmul(
            ab[:, TB:2 * TB], C["ones_row"], brow[:, msl],
            start=True, stop=True),
            waits=(row_done, sq_done, done), inc=ab_done)
        for f in range(NF):
            fsl = slice(f * TB, (f + 1) * TB)
            sl = slice(tb * TB, (tb + 1) * TB)
            p.on("vector",
                 lambda e, f=f, w=w, fsl=fsl: e.tensor_tensor(
                     sq[:, fsl], xw[w][f][:, :], ab[:, 0:TB], ALU.mult),
                 waits=(ab_done,), inc=sq_done)
            if dst_sb is not None:
                p.on("vector",
                     lambda e, f=f, sl=sl, fsl=fsl: e.tensor_tensor(
                         dst_sb[f][:, sl], sq[:, fsl], ab[:, TB:2 * TB],
                         ALU.add),
                     waits=(sq_done,), inc=done)
            if dst_d is not None or sums_sb is not None:
                p.on("vector",
                     lambda e, f=f, fsl=fsl: e.tensor_tensor(
                         sq[:, fsl], sq[:, fsl], ab[:, TB:2 * TB], ALU.add),
                     waits=((sq_done, zero) if sums_sb is not None
                            else (sq_done,)), inc=done)
                if dst_d is not None:
                    p.dma("sync",
                          lambda f=f, sl=sl: dst_d[f * 128:(f + 1) * 128, sl],
                          lambda f=f, fsl=fsl: sq[:, fsl], st, waits=(done,))
                if sums_sb is not None:
                    p.on("vector",
                         lambda e, f=f, fsl=fsl: e.reduce_sum(
                             red[:, :], sq[:, fsl], AX.X),
                         waits=(done,), inc=done)
                    p.on("vector",
                         lambda e, f=f: e.tensor_tensor(
                             sums_sb[:, f:f + 1], sums_sb[:, f:f + 1],
                             red[:, :], ALU.add),
                         waits=(done,), inc=done)
    ps_stack.close()


def emit_twgroup(p, C, src_fn, t_ps, tt_sb, dst_ap_fn, tmm, tev,
                 wctr, waits=(), n128=4, wcols=128):
    for i in range(n128):
        p.on("tensor",
             lambda e, i=i: e.transpose(
                 t_ps[:, :].bitcast(BF16)[:, 0:128],
                 src_fn(i), C["ident"]),
             waits=tuple(waits) + (tev,), inc=tmm)
        p.on("vector",
             lambda e, i=i: e.tensor_copy(
                 tt_sb[:, i * 128:(i + 1) * 128],
                 t_ps[:, :].bitcast(BF16)[:, 0:128]),
             waits=(tmm,), inc=tev)
    p.dma("sync", dst_ap_fn,
          lambda: tt_sb[:, 0:n128 * 128].rearrange(
              "p (i c) -> p i c", c=128)[:, :, 0:wcols],
          wctr, waits=(tev,))


def tok_view(stage, tb, col0, wcols, n128=4):
    return stage[tb * TB:(tb + 1) * TB, col0:col0 + wcols].rearrange(
        "(i p) c -> p i c", p=128)



def build_fused(upto=99, probes=()):
    p = Prog()
    nc = p.nc

    x_in = nc.dram_tensor("x_my", [D, CF], F32, kind="ExternalInput")
    meta_in = nc.dram_tensor("meta", [1, MMAX], I32, kind="ExternalInput")
    out_d = nc.dram_tensor("x8_o", [D, CF], F32, kind="ExternalOutput")

    qkvw_d = nc.dram_tensor("qkv_w", [D, 3 * D], F32, kind="ExternalInput")
    qkvb_d = nc.dram_tensor("qkv_b_cols", [128, 24], F32, kind="ExternalInput")
    pw_d = nc.dram_tensor("proj_w", [D, D], F32, kind="ExternalInput")
    pb_d = nc.dram_tensor("proj_b_cols", [128, NF], F32, kind="ExternalInput")
    wkqv_d = nc.dram_tensor("Wkqv", [16, D, 2112], F32, kind="ExternalInput")
    bkqv_d = nc.dram_tensor("bkqv_cols", [16, 128, 17], F32,
                            kind="ExternalInput")
    wp_d = nc.dram_tensor("Wp", [D, D], F32, kind="ExternalInput")
    bp_d = nc.dram_tensor("bp_cols", [128, NF], F32, kind="ExternalInput")
    msk_d = nc.dram_tensor("mask", [128, 8 * TB], F32, kind="ExternalInput")
    cw1_d = nc.dram_tensor("cn_w1", [D, D], F32, kind="ExternalInput")
    cw2_d = nc.dram_tensor("cn_w2", [D, 512], F32, kind="ExternalInput")
    cw3_d = nc.dram_tensor("cn_w3", [512, 256], F32, kind="ExternalInput")
    cw4_d = nc.dram_tensor("cn_w4", [256, 1], F32, kind="ExternalInput")
    gw1_d = nc.dram_tensor("gc_w1", [D, 512], F32, kind="ExternalInput")
    gw2_d = nc.dram_tensor("gc_w2", [512, 256], F32, kind="ExternalInput")
    gw3_d = nc.dram_tensor("gc_w3", [256, 1], F32, kind="ExternalInput")
    cb1_d = nc.dram_tensor("cn_b1_cols", [128, 8], F32, kind="ExternalInput")
    cb2_d = nc.dram_tensor("cn_b2_cols", [128, 4], F32, kind="ExternalInput")
    cb3_d = nc.dram_tensor("cn_b3_cols", [128, 2], F32, kind="ExternalInput")
    cb4_d = nc.dram_tensor("cn_b4", [1, 1], F32, kind="ExternalInput")
    gb1_d = nc.dram_tensor("gc_b1_cols", [128, 4], F32, kind="ExternalInput")
    gb2_d = nc.dram_tensor("gc_b2_cols", [128, 2], F32, kind="ExternalInput")
    gb3_d = nc.dram_tensor("gc_b3", [1, 1], F32, kind="ExternalInput")
    ar_d = nc.dram_tensor("ar", [1, 1], F32, kind="ExternalInput")
    wq_d = nc.dram_tensor("Wq", [D, D], F32, kind="ExternalInput")
    wk_d = nc.dram_tensor("Wk", [D, D], F32, kind="ExternalInput")
    wv_d = nc.dram_tensor("Wv", [D, D], F32, kind="ExternalInput")
    bq_d = nc.dram_tensor("bq_cols", [128, NF], F32, kind="ExternalInput")
    bk_d = nc.dram_tensor("bk_cols", [128, NF], F32, kind="ExternalInput")
    bv_d = nc.dram_tensor("bv_cols", [128, NF], F32, kind="ExternalInput")
    wo_d = nc.dram_tensor("Wo", [D, D], F32, kind="ExternalInput")
    bo_d = nc.dram_tensor("bo_cols", [128, NF], F32, kind="ExternalInput")
    g2_d = nc.dram_tensor("g2lhs", [128, 112], F32, kind="ExternalInput")
    g1_d = nc.dram_tensor("g1rhs", [256, 32], F32, kind="ExternalInput")
    br_d = nc.dram_tensor("br_cols", [128, 4], F32, kind="ExternalInput")
    bi_d = nc.dram_tensor("bi_cols", [128, 4], F32, kind="ExternalInput")
    wqkv1_d = nc.dram_tensor("w_qkv", [D, 3 * D], F32, kind="ExternalInput")
    wout_d = nc.dram_tensor("w_out", [D, D], F32, kind="ExternalInput")
    bout_d = nc.dram_tensor("bout_cols", [128, NF], F32, kind="ExternalInput")

    def loc(name, dt=F32):
        return nc.dram_tensor(name, [D, CF], dt)
    xA = loc("xA")
    xB = loc("xB")
    sA = loc("sA")
    sB = loc("sB")
    bA = loc("bA", BF16)
    bB = loc("bB", BF16)
    x1_l, x2_l, x3_l, x4_l, x5_l, x6_l, x7_l = xA, xB, xA, xB, xA, xB, xA
    s0_l, s3_l, s5_l, s6_l, s7_l = sA, sB, sA, sB, sA
    qn_l, q1_l, phq_l = bA, bA, bA
    sa_l, ao1_l, ao2_l = bB, bB, bB
    kv_dbg = nc.dram_tensor("kv_dbg", [128, 1024], F32)
    fl_dbg = nc.dram_tensor("fl_dbg", [1, 16], I32)
    stg1 = nc.dram_tensor("stg1", [CF, D], BF16)
    stg2 = nc.dram_tensor("stg2", [CF, D], BF16)
    stgF = nc.dram_tensor("stgF", [D, CF], BF16)
    kn_s, vn_s = stg1, stg2
    k1_s, v1_s = stgF, stg1
    pk_s, v2_s = stg1, stg2
    ks_one = nc.dram_tensor("ks_s", [8 * D, CF], BF16)
    ks_s = [ks_one, ks_one]
    vs_s = nc.dram_tensor("vs_s", [16 * CF, 64], BF16)
    q2_l = nc.dram_tensor("q2_l", [16, D, CF], BF16)

    p.fl = nc.dram_tensor("sh_flag", [1, 512], I32, addr_space="Shared")
    p.zl = nc.dram_tensor("sh_zero", [1, 512], I32, addr_space="Shared")
    shT1 = nc.dram_tensor("sh_t1", [SEQ, D], BF16, addr_space="Shared")
    shT2 = nc.dram_tensor("sh_t2", [SEQ, D], BF16, addr_space="Shared")
    kn_t, v1t = shT1, shT1
    vn_t, v2t = shT2, shT2
    sums1 = nc.dram_tensor("sh_sums1", [2, 128, NF], F32, addr_space="Shared")
    sums2 = nc.dram_tensor("sh_sums2", [2, 128, NF], F32, addr_space="Shared")
    KS = [nc.dram_tensor(f"sh_K{i}", [8, D, SEQ], BF16, addr_space="Shared")
          for i in range(2)]
    VS = nc.dram_tensor("sh_V", [16, SEQ, 64], BF16, addr_space="Shared")
    k1f = nc.dram_tensor("sh_k1f", [D, SEQ], BF16, addr_space="Shared")
    pkt = nc.dram_tensor("sh_pkt", [SEQ, D], BF16, addr_space="Shared")

    p.mt_s = p.sbuf("mt", [1, MMAX], I32, prog=True)
    p.fv_t = p.sbuf("fv", [1, 8], I32, prog=True)
    p.zv = p.sbuf("zv", [1, 4], I32, prog=True)
    p.dly_sb = p.sbuf("dly", [128, 1024], BF16, prog=True)
    p.dly_src = nc.dram_tensor("dly_d", [128, 1024], BF16)
    p.zd = p._sem("zd")
    p.mld = p.ctr("mld")
    ld = p.ctr("ld")
    st = p.ctr("st")
    p.dma("sync", lambda: p.mt_s[:, :], lambda: meta_in[:, :], p.mld)
    C = load_consts(p, ld)
    wt_sb = [[p.sbuf(f"wt{ri}_{t}", [128, 512], BF16, prog=True)
              for t in range(4)] for ri in range(2)]
    wtn_sb = [p.sbuf(f"wtn{t}", [128, 512], BF16, prog=True)
              for t in range(4)]

    def phA():
        wld = p.ctr("wld")
        wslab = emit_wslabs(p, qkvw_d, NF, 3 * D, wld, "aqkv")
        emit_zero_handshake(p)

        bias_sb = p.sbuf("abias", [128, 24], F32)
        p.dma("scalar", lambda: bias_sb[:, :], lambda: qkvb_d[:, :], ld)
        xn_sb = [p.sbuf(f"xn{f}", [128, CF], BF16) for f in range(NF)]
        ln_done = p.ctr("lnd")
        ln_stream(p, C, x_in, ln_done, st, dst_sb=xn_sb)

        y_ps = [p.psum(f"yA{t}", [128, TB], F32) for t in range(4)]
        nsq = p.psum("nsq", [32, TB], F32)
        bc = p.psum("bcA", [128, TB], F32)
        t_ps = p.psum("tpsA", [128, 64], F32)
        qk_f32 = p.sbuf("qk_f32", [128, TB], F32)
        sqt = p.sbuf("sqtA", [128, TB], F32)
        rnorm = p.sbuf("rnA", [32, TB], F32)
        out_bb = p.sbuf("obbA", [128, TB], BF16)
        tt_sb = p.sbuf("ttA", [128, TB], BF16)
        mm = p.ctr("mm")
        ev = p.ctr("ev")
        tmm = p.ctr("tmm")
        tev = p.ctr("tev")

        for m in range(24):
            s_part = m // NF
            fo = m % NF
            for tb in range(NTB):
                sl = slice(tb * TB, (tb + 1) * TB)
                for k in range(NF):
                    waits = [ln_done, (wld.sem, 16 * (k + 1))]
                    if k == 0:
                        waits.append(ev)
                    p.on("tensor",
                         lambda e, m=m, k=k, tb=tb, sl=sl: e.matmul(
                             y_ps[tb % 4][:, :],
                             wslab[k][:, m * 128:(m + 1) * 128],
                             xn_sb[k][:, sl],
                             start=(k == 0), stop=(k == NF - 1)),
                         waits=waits, inc=(mm if k == NF - 1 else None))
                bias_ap = bias_sb[:, m:m + 1]
                if s_part == 2:
                    p.on("scalar",
                         lambda e, tb=tb, bias_ap=bias_ap: e.activation(
                             out_bb[:, :], y_ps[tb % 4][:, :], AF.Identity,
                             bias=bias_ap),
                         waits=(mm, st, tev), inc=ev)
                    emit_twgroup(
                        p, C, lambda i: out_bb[:, i * 128:(i + 1) * 128],
                        t_ps, tt_sb,
                        lambda fo=fo, tb=tb: tok_view(vn_s, tb, fo * 128, 128),
                        tmm, tev, st, waits=(ev,))
                else:
                    p.on("scalar",
                         lambda e, tb=tb, bias_ap=bias_ap: e.activation(
                             qk_f32[:, :], y_ps[tb % 4][:, :], AF.Identity,
                             bias=bias_ap),
                         waits=(mm, st, ev, tev), inc=ev)
                    p.on("scalar",
                         lambda e: e.activation(sqt[:, :], qk_f32[:, :],
                                                AF.Square),
                         waits=(ev,), inc=ev)
                    p.on("tensor",
                         lambda e: e.matmul(nsq[0:2, :], C["sel64"], sqt[:, :],
                                            start=True, stop=True),
                         waits=(ev,), inc=mm)
                    p.on("scalar",
                         lambda e: e.activation(rnorm[0:2, :], nsq[0:2, :],
                                                AF.Sqrt),
                         waits=(mm,), inc=ev)
                    p.on("vector",
                         lambda e: e.tensor_scalar_max(rnorm[0:2, :],
                                                       rnorm[0:2, :], 1e-12),
                         waits=(ev,), inc=ev)
                    p.on("vector",
                         lambda e: e.reciprocal(rnorm[0:2, :], rnorm[0:2, :]),
                         waits=(ev,), inc=ev)
                    p.on("tensor",
                         lambda e: e.matmul(bc[:, :], C["sel64T"], rnorm[0:2, :],
                                            start=True, stop=True),
                         waits=(ev,), inc=mm)
                    p.on("vector",
                         lambda e: e.tensor_tensor(out_bb[:, :], qk_f32[:, :],
                                                   bc[:, :], ALU.mult),
                         waits=(mm, tev), inc=ev)
                    if s_part == 0:
                        p.dma("sync",
                              lambda fo=fo, sl=sl: qn_l[fo * 128:(fo + 1) * 128,
                                                        sl],
                              lambda: out_bb[:, :], st, waits=(ev,))
                    else:
                        emit_twgroup(
                            p, C, lambda i: out_bb[:, i * 128:(i + 1) * 128],
                            t_ps, tt_sb,
                            lambda fo=fo, tb=tb: tok_view(kn_s, tb, fo * 128,
                                                          128),
                            tmm, tev, st, waits=(ev,))
        for stage, shd in ((kn_s, kn_t), (vn_s, vn_t)):
            p.dma_wreg("sync", shd, [[2 * TB * D, NTB], [D, TB], [1, D]],
                       q0(0, 0) * D, q0(0, 1) * D,
                       lambda stage=stage: stage[:, :].rearrange(
                           "(tb r) d -> tb r d", r=TB),
                       st, waits=(st,))
    phA()
    p.barrier()

    if upto < 1:
        return finish(p, nc, probes, locals())

    def phB():
        mm = p.ctr("mm")
        ev = p.ctr("ev")
        bias_sb = p.sbuf("bbias", [128, NF], F32)
        ao_sb = [p.sbuf(f"aoB{f}", [128, CF], BF16) for f in range(NF)]
        kv_sb = p.sbuf("kvB", [128, 16 * 64], BF16)
        pslab = emit_wslabs(p, pw_d, NF, D, p.ctr("wld"), "bpj")
        sd1 = p.pair_sync(1)
        dbg = p.ctr("dbg")
        if "kvdelay" in SKIP:
            dly_l = nc.dram_tensor("dly_l", [128, 4096], F32)
            for i in range(64):
                p.dma("gpsimd", lambda: dly_l[:, :],
                      lambda: x_in[0:128, :], dbg,
                      waits=[(sd1, 1)])
        kvdel = (dbg.sem, dbg.v)
        p.dma("gpsimd", lambda: fl_dbg[0:1, 0:1], lambda: p.fl[0:1, 0:1],
              dbg, waits=[(sd1, 1)])
        p.dma("gpsimd", lambda: fl_dbg[0:1, 1:2], lambda: p.fl[0:1, 256:257],
              dbg)
        p.dma("gpsimd", lambda: fl_dbg[0:1, 2:3], lambda: p.zl[0:1, 0:1],
              dbg)
        p.dma("gpsimd", lambda: fl_dbg[0:1, 3:4], lambda: p.zl[0:1, 256:257],
              dbg)
        p.dma("gpsimd", lambda: fl_dbg[0:1, 4:6], lambda: p.mt_s[0:1, 4:6],
              dbg)
        p.dma("gpsimd", lambda: fl_dbg[0:1, 6:8], lambda: p.fv_t[0:1, 0:2],
              dbg)
        p.dma("scalar", lambda: bias_sb[:, :], lambda: pb_d[:, :], ld)
        sub = ExitStack()
        old_stack = p.phase_stack
        p.phase_stack = sub
        qn_sb = [p.sbuf(f"qnB{f}", [128, CF], BF16) for f in range(NF)]
        ktile = [p.sbuf(f"ktB{i}", [128, D], BF16) for i in range(4)]
        vtile = [p.sbuf(f"vtB{i}", [128, D], BF16) for i in range(4)]
        xr_sb = [p.sbuf(f"xrB{i}", [128, TB], F32) for i in range(2)]
        xo_sb = [p.sbuf(f"xoB{i}", [128, TB], F32) for i in range(2)]
        kv_acc = p.sbuf("kvacc", [64, 16 * 64], F32)
        kv_ps = p.psum("kvps", [64, 16 * 64], F32)
        op_ps = p.psum("opB", [64, TB], F32)
        y_ps = [p.psum(f"yB{t}", [128, TB], F32) for t in range(4)]
        p.phase_stack = old_stack
        qld = p.ctr("qld")
        for f in range(NF):
            p.dma("sync", lambda f=f: qn_sb[f][:, :],
                  lambda f=f: qn_l[f * 128:(f + 1) * 128, :], qld)
        kld = p.ctr("kld")
        kvmm = p.ctr("kvmm")
        kvev = p.ctr("kvev")
        zacc = p.ctr("zacc")
        p.on("vector", lambda e: e.memset(kv_acc[:, :], 0.0), inc=zacc)
        NCH = 4
        for ch in range(NKT // NCH):
            for i in range(NCH):
                tt = ch * NCH + i
                waits = [(sd1, 1), kvdel]
                if ch > 0:
                    waits.append((kvmm.sem, 16 * ch))
                p.on("sync",
                     lambda e, i=i, tt=tt: e.dma_start(
                         ktile[i][:, :], kn_t[tt * 128:(tt + 1) * 128, :]),
                     waits=waits, inc=kld, inc_k=16)
                p.on("sync",
                     lambda e, i=i, tt=tt: e.dma_start(
                         vtile[i][:, :], vn_t[tt * 128:(tt + 1) * 128, :]),
                     waits=waits, inc=kld, inc_k=16)
            kldnow = (kld.sem, kld.v)
            for h in range(16):
                for i in range(NCH):
                    waits = [kldnow]
                    if h == 0 and i == 0:
                        waits.append((kvev.sem, ch))
                    p.on("tensor",
                         lambda e, i=i, h=h: e.matmul(
                             kv_ps[0:64, h * 64:(h + 1) * 64],
                             ktile[i][:, h * 64:(h + 1) * 64],
                             vtile[i][:, h * 64:(h + 1) * 64],
                             start=(i == 0), stop=(i == NCH - 1),
                             skip_group_check=True),
                         waits=waits,
                         inc=(kvmm if i == NCH - 1 else None))
            p.on("vector",
                 lambda e: e.tensor_tensor(kv_acc[:, :], kv_acc[:, :],
                                           kv_ps[:, :], ALU.add),
                 waits=((kvmm.sem, 16 * (ch + 1)), zacc), inc=kvev)
        p.on("vector", lambda e: e.tensor_copy(kv_sb[0:64, :], kv_acc[:, :]),
             waits=(kvev,), inc=kvev)
        p.on("vector",
             lambda e: e.tensor_copy(kv_sb[64:128, :], kv_acc[:, :]),
             waits=(kvev,), inc=kvev)
        p.dma("gpsimd", lambda: kv_dbg[:, :], lambda: kv_sb[:, :], st,
              waits=(kvev,))

        aomm = p.ctr("aomm")
        aoev = p.ctr("aoev")
        for h in range(16):
            fo, po_ = h // 2, (h % 2) * 64
            for tb in range(NTB):
                sl = slice(tb * TB, (tb + 1) * TB)
                p.on("tensor",
                     lambda e, h=h, fo=fo, po_=po_, sl=sl: e.matmul(
                         op_ps[:, :], kv_sb[po_:po_ + 64, h * 64:(h + 1) * 64],
                         qn_sb[fo][po_:po_ + 64, sl], start=True, stop=True),
                     waits=(kvev, qld, aoev), inc=aomm)
                p.on("vector",
                     lambda e, fo=fo, po_=po_, sl=sl: e.tensor_copy(
                         ao_sb[fo][po_:po_ + 64, sl], op_ps[:, :]),
                     waits=(aomm,), inc=aoev)

        xld = p.ctr("xld")

        def evacB(m, tb):
            sl = slice(tb * TB, (tb + 1) * TB)
            w = (m * NTB + tb) % 2
            p.dma("sync", lambda w=w: xr_sb[w][:, :],
                  lambda m=m, sl=sl: x_in[m * 128:(m + 1) * 128, sl],
                  xld, waits=(ev,))
            xnow = (xld.sem, xld.v)
            p.on("vector",
                 lambda e, m=m, w=w: e.scalar_tensor_tensor(
                     xo_sb[w][:, :], y_ps[tb % 4][:, :], bias_sb[:, m:m + 1],
                     xr_sb[w][:, :], ALU.add, ALU.add),
                 waits=(mm, xnow, st), inc=ev)
            p.dma("sync", lambda m=m, sl=sl: x1_l[m * 128:(m + 1) * 128, sl],
                  lambda w=w: xo_sb[w][:, :], st, waits=(ev,))

        emit_mm_chain(
            p, lambda k, m: pslab[k][:, m * 128:(m + 1) * 128],
            lambda k, tb: ao_sb[k][:, tb * TB:(tb + 1) * TB],
            y_ps, NF, NF, NTB, mm, ev, [aoev, ld], evacB)

        sub.close()
        sums_sb = p.sbuf("sumsB", [128, NF], F32)
        ln2 = p.ctr("ln2")
        ln_stream(p, C, x1_l, ln2, st, dst_d=s0_l, sums_sb=sums_sb,
                  pre=((aomm.sem, aomm.v), ev))
        p.dma_wreg("sync", sums1, [[NF, 128], [1, NF]], 0, 128 * NF,
                   lambda: sums_sb[:, :], st, waits=(ln2,))
    phB()
    p.barrier()

    if upto < 2:
        return finish(p, nc, probes, locals())

    def emit_D(s_d, x_d, sums_sh, xout_d, sync_k):
        wldD = p.ctr("wldD")
        cw1 = emit_wslabs(p, cw1_d, 8, D, wldD, "cw1")
        cw2 = emit_wslabs(p, cw2_d, 8, 512, wldD, "cw2")
        cw3 = emit_wslabs(p, cw3_d, 4, 256, wldD, "cw3")
        cw4 = emit_wslabs(p, cw4_d, 2, 1, wldD, "cw4")
        gw1 = emit_wslabs(p, gw1_d, 8, 512, wldD, "gw1")
        gw2 = emit_wslabs(p, gw2_d, 4, 256, wldD, "gw2")
        gw3 = emit_wslabs(p, gw3_d, 2, 1, wldD, "gw3")
        sd_sums = p.pair_sync(sync_k)
        biases = p.sbuf("biasD", [128, 32], F32)
        rowsc = p.sbuf("rowscD", [1, 16], F32)
        p.dma("scalar", lambda: biases[:, 0:8], lambda: cb1_d[:, :], ld)
        p.dma("scalar", lambda: biases[:, 8:12], lambda: cb2_d[:, :], ld)
        p.dma("scalar", lambda: biases[:, 12:14], lambda: cb3_d[:, :], ld)
        p.dma("scalar", lambda: biases[:, 16:20], lambda: gb1_d[:, :], ld)
        p.dma("scalar", lambda: biases[:, 20:22], lambda: gb2_d[:, :], ld)
        p.dma("scalar", lambda: rowsc[:, 0:1], lambda: cb4_d[:, :], ld)
        p.dma("scalar", lambda: rowsc[:, 1:2], lambda: gb3_d[:, :], ld)
        p.dma("scalar", lambda: rowsc[:, 2:3], lambda: ar_d[:, :], ld)

        sums_sb = p.sbuf("sumsD", [128, 2 * NF], F32)
        for i in range(2):
            p.dma("sync", lambda i=i: sums_sb[:, i * NF:(i + 1) * NF],
                  lambda i=i: sums_sh[i, :, :], ld,
                  waits=((sd_sums, 1),))
        gc = p.sbuf("gcD", [128, NF], F32)
        gcb = p.sbuf("gcbD", [128, NF], BF16)
        gev = p.ctr("gev")
        gmm = p.ctr("gmm")
        ldnow = (ld.sem, ld.v)
        p.on("vector", lambda e: e.tensor_tensor(
            gc[:, :], sums_sb[:, 0:NF], sums_sb[:, NF:2 * NF], ALU.add),
            waits=(ldnow,), inc=gev)
        p.on("vector", lambda e: e.tensor_scalar_mul(gc[:, :], gc[:, :],
                                                     1.0 / SEQ),
             waits=(gev,), inc=gev)
        p.on("vector", lambda e: e.tensor_copy(gcb[:, :], gc[:, :]),
             waits=(gev,), inc=gev)
        g1_ps = p.psum("g1D", [128, 4], F32)
        g2_ps = p.psum("g2D", [128, 2], F32)
        g3_ps = p.psum("g3D", [1, 1], F32)
        g1b = p.sbuf("g1bD", [128, 4], BF16)
        g2b = p.sbuf("g2bD", [128, 2], BF16)
        for m in range(4):
            for k in range(8):
                p.on("tensor",
                     lambda e, m=m, k=k: e.matmul(
                         g1_ps[:, m:m + 1], gw1[k][:, m * 128:(m + 1) * 128],
                         gcb[:, k:k + 1], start=(k == 0), stop=(k == 7),
                         skip_group_check=True),
                     waits=(gev, (wldD.sem, wldD.v)),
                     inc=(gmm if k == 7 else None))
        for m in range(4):
            p.on("scalar",
                 lambda e, m=m: e.activation(
                     g1b[:, m:m + 1], g1_ps[:, m:m + 1], AF.Gelu,
                     bias=biases[:, 16 + m:17 + m]),
                 waits=(gmm,), inc=gev)
        for m in range(2):
            for k in range(4):
                p.on("tensor",
                     lambda e, m=m, k=k: e.matmul(
                         g2_ps[:, m:m + 1], gw2[k][:, m * 128:(m + 1) * 128],
                         g1b[:, k:k + 1], start=(k == 0), stop=(k == 3),
                         skip_group_check=True),
                     waits=(gev,), inc=(gmm if k == 3 else None))
        for m in range(2):
            p.on("scalar",
                 lambda e, m=m: e.activation(
                     g2b[:, m:m + 1], g2_ps[:, m:m + 1], AF.Gelu,
                     bias=biases[:, 20 + m:21 + m]),
                 waits=(gmm,), inc=gev)
        for k in range(2):
            p.on("tensor",
                 lambda e, k=k: e.matmul(g3_ps[:, :], gw3[k][:, 0:1],
                                         g2b[:, k:k + 1], start=(k == 0),
                                         stop=(k == 1)),
                 waits=(gev,), inc=(gmm if k == 1 else None))
        p.on("scalar", lambda e: e.activation(rowsc[:, 3:4], g3_ps[:, :],
                                              AF.Sigmoid, bias=rowsc[:, 1:2]),
             waits=(gmm,), inc=gev)
        p.on("vector", lambda e: e.tensor_scalar_mul(rowsc[:, 4:5],
                                                     rowsc[:, 3:4], 0.7),
             waits=(gev,), inc=gev)
        p.on("vector", lambda e: e.tensor_scalar_mul(rowsc[:, 5:6],
                                                     rowsc[:, 2:3], 2.0),
             waits=(gev,), inc=gev)
        p.on("vector", lambda e: e.tensor_scalar(rowsc[:, 6:7], rowsc[:, 2:3],
                                                 -1.0, 0.1, ALU.mult,
                                                 ALU.add),
             waits=(gev,), inc=gev)
        gf_done = p.ctr("gfd")
        p.on("vector", lambda e: e.tensor_copy(rowsc[:, 7:8], rowsc[:, 6:7]),
             waits=(gev,), inc=gf_done)

        TH = 512
        HTB = TH // TB
        sa = [p.sbuf(f"saD{f}", [128, TH], F32) for f in range(NF)]
        sb = [p.sbuf(f"sbD{f}", [128, TH], F32) for f in range(NF)]
        sbf = [p.sbuf(f"sbfD{f}", [128, TH], BF16) for f in range(NF)]
        h1b = [p.sbuf(f"h1bD{f}", [128, TH], BF16) for f in range(NF)]
        h2b = [p.sbuf(f"h2bD{f}", [128, TH], BF16) for f in range(4)]
        h3b = [p.sbuf(f"h3bD{f}", [128, TH], BF16) for f in range(2)]
        x1h = [p.sbuf(f"x1hD{f}", [128, TH], F32) for f in range(NF)]
        tfr = p.sbuf("tfrD", [1, TH], F32)
        cplr = p.sbuf("cplrD", [1, TH], F32)
        y_ps = [p.psum(f"ypD{t}", [128, TB], F32) for t in range(2)]
        t4_ps = p.psum("t4D", [1, TB], F32)
        cp_ps = p.psum("cpD", [128, 2 * TB], F32)
        hld = p.ctr("hld")
        mmD = p.ctr("mm")
        evD = p.ctr("ev")
        stD = st
        TWIST = {0: (4, 1.0), 1: (5, 1.0), 2: (6, -1.0), 3: (7, -1.0),
                 4: (0, -1.0), 5: (1, -1.0), 6: (2, 1.0), 7: (3, 1.0)}

        for half in range(CF // TH):
            hsl = slice(half * TH, (half + 1) * TH)
            for f in range(NF):
                p.on("sync",
                     lambda e, f=f, hsl=hsl: e.dma_start(
                         sa[f][:, :], s_d[f * 128:(f + 1) * 128, hsl]),
                     waits=(evD, stD), inc=hld, inc_k=16)
                p.on("sync",
                     lambda e, f=f, hsl=hsl: e.dma_start(
                         x1h[f][:, :], x_d[f * 128:(f + 1) * 128, hsl]),
                     waits=(evD, stD), inc=hld, inc_k=16)
            cast0 = p.ctr("cst")
            for f in range(NF):
                p.on("vector",
                     lambda e, f=f: e.tensor_copy(sbf[f][:, :], sa[f][:, :]),
                     waits=(hld, evD), inc=cast0)
            cur, nxt = sa, sb
            for cyc in range(3):
                def evac_h1(m, tb):
                    p.on("scalar",
                         lambda e, m=m, tb=tb: e.activation(
                             h1b[m][:, tb * TB:(tb + 1) * TB],
                             y_ps[tb % 2][:, :],
                             AF.Gelu, bias=biases[:, m:m + 1]),
                         waits=(mmD,), inc=evD)
                emit_mm_chain(p, lambda k, m: cw1[k][:, m * 128:(m + 1) * 128],
                              lambda k, tb: sbf[k][:, tb * TB:(tb + 1) * TB],
                              y_ps, 8, 8, HTB, mmD, evD,
                              [cast0, (wldD.sem, wldD.v)], evac_h1)

                def evac_h2(m, tb):
                    p.on("scalar",
                         lambda e, m=m, tb=tb: e.activation(
                             h2b[m][:, tb * TB:(tb + 1) * TB],
                             y_ps[tb % 2][:, :],
                             AF.Gelu, bias=biases[:, 8 + m:9 + m]),
                         waits=(mmD,), inc=evD)
                emit_mm_chain(p, lambda k, m: cw2[k][:, m * 128:(m + 1) * 128],
                              lambda k, tb: h1b[k][:, tb * TB:(tb + 1) * TB],
                              y_ps, 8, 4, HTB, mmD, evD, [evD], evac_h2)

                def evac_h3(m, tb):
                    p.on("scalar",
                         lambda e, m=m, tb=tb: e.activation(
                             h3b[m][:, tb * TB:(tb + 1) * TB],
                             y_ps[tb % 2][:, :],
                             AF.Gelu, bias=biases[:, 12 + m:13 + m]),
                         waits=(mmD,), inc=evD)
                emit_mm_chain(p, lambda k, m: cw3[k][:, m * 128:(m + 1) * 128],
                              lambda k, tb: h2b[k][:, tb * TB:(tb + 1) * TB],
                              y_ps, 4, 2, HTB, mmD, evD, [evD], evac_h3)
                for tb in range(HTB):
                    for k in range(2):
                        p.on("tensor",
                             lambda e, k=k, tb=tb: e.matmul(
                                 t4_ps[:, :], cw4[k][:, 0:1],
                                 h3b[k][:, tb * TB:(tb + 1) * TB],
                                 start=(k == 0), stop=(k == 1)),
                             waits=(evD,), inc=(mmD if k == 1 else None))
                    p.on("scalar",
                         lambda e, tb=tb: e.activation(
                             tfr[:, tb * TB:(tb + 1) * TB], t4_ps[:, :],
                             AF.Sigmoid, bias=rowsc[:, 0:1]),
                         waits=(mmD,), inc=evD)
                p.on("vector",
                     lambda e: e.tensor_scalar(cplr[:, :], tfr[:, :], 0.3,
                                               rowsc[:, 4:5], ALU.mult,
                                               ALU.add),
                     waits=(evD, gf_done), inc=evD)
                p.on("vector",
                     lambda e: e.tensor_scalar(cplr[:, :], cplr[:, :],
                                               rowsc[:, 5:6], rowsc[:, 6:7],
                                               ALU.mult, ALU.add),
                     waits=(evD,), inc=evD)
                for tb in range(HTB):
                    p.on("tensor",
                         lambda e, tb=tb: e.matmul(
                             cp_ps[:, tb * TB:(tb + 1) * TB], C["ones_row"],
                             cplr[:, tb * TB:(tb + 1) * TB],
                             start=True, stop=True),
                         waits=(evD,), inc=mmD)
                tw = p.ctr("tw")
                for f in range(NF):
                    src, sgn = TWIST[f]
                    for tb in range(HTB):
                        tsl = slice(tb * TB, (tb + 1) * TB)
                        p.on("vector",
                             lambda e, f=f, src=src, tsl=tsl, cur=cur,
                             nxt=nxt: e.tensor_tensor(
                                 nxt[f][:, tsl],
                                 cur[src][:, tsl], cp_ps[:, tsl], ALU.mult),
                             waits=(mmD, stD), inc=tw)
                        p.on("vector",
                             lambda e, f=f, sgn=sgn, tsl=tsl, nxt=nxt:
                                 e.tensor_scalar_mul(nxt[f][:, tsl],
                                                     nxt[f][:, tsl], sgn),
                             waits=(tw,), inc=tw)
                        p.on("vector",
                             lambda e, f=f, tsl=tsl, cur=cur, nxt=nxt:
                                 e.tensor_tensor(
                                     nxt[f][:, tsl], nxt[f][:, tsl],
                                     cur[f][:, tsl], ALU.add),
                             waits=(tw,), inc=tw)
                        if cyc < 2:
                            p.on("vector",
                                 lambda e, f=f, tsl=tsl, nxt=nxt:
                                     e.tensor_copy(sbf[f][:, tsl],
                                                   nxt[f][:, tsl]),
                                 waits=(tw,), inc=evD)
                        else:
                            p.on("vector", lambda e: e.nop(), waits=(tw,),
                                 inc=evD)
                cur, nxt = nxt, cur
            for f in range(NF):
                p.on("vector",
                     lambda e, f=f, cur=cur: e.tensor_tensor(
                         cur[f][:, :], cur[f][:, :], x1h[f][:, :], ALU.add),
                     waits=(evD, hld), inc=evD)
                p.dma("sync",
                      lambda f=f, hsl=hsl: xout_d[f * 128:(f + 1) * 128, hsl],
                      lambda f=f, cur=cur: (cur[f][:, :]), stD, waits=(evD,))

    emit_D(s0_l, x1_l, sums1, x2_l, 2)
    p.barrier()
    st = p.ctr("st2")

    if upto < 3:
        return finish(p, nc, probes, locals())

    def phE():
        wldE = p.ctr("wldE")
        slabE = [p.sbuf(f"wkE{k}", [128, 2112], BF16) for k in range(NF)]
        for k in range(NF):
            p.on("gpsimd",
                 lambda e, k=k: e.dma_start(
                     slabE[k][:, :], wkqv_d[0, k * 128:(k + 1) * 128, :]),
                 inc=wldE, inc_k=16)
        xnE = [p.sbuf(f"xnE{f}", [128, CF], BF16) for f in range(NF)]
        lnE = p.ctr("lnd")
        ln_stream(p, C, x2_l, lnE, st, dst_sb=xnE)
        biasE = p.sbuf("biasE", [128, 16 * 17], F32)
        for h in range(16):
            p.dma("scalar", lambda h=h: biasE[:, h * 17:(h + 1) * 17],
                  lambda h=h: bkqv_d[h, :, :], ld)
        y_ps = [p.psum(f"yE{t}", [128, TB], F32) for t in range(4)]
        t_ps = p.psum("tpsE", [128, 64], F32)
        out_bb = p.sbuf("obbE", [128, TB], BF16)
        tt_sb = p.sbuf("ttE", [128, TB], BF16)
        mm = p.ctr("mm")
        ev = p.ctr("ev")
        tmm = p.ctr("tmm")
        tev = p.ctr("tev")
        ks0_done = ()
        for h in range(16):
            if h > 0:
                for k in range(NF):
                    p.on("gpsimd",
                         lambda e, h=h, k=k: e.dma_start(
                             slabE[k][:, :],
                             wkqv_d[h, k * 128:(k + 1) * 128, :]),
                         waits=(mm,), inc=wldE, inc_k=16)
            wnow = (wldE.sem, 16 * NF * (h + 1))
            sl_ = slabE
            for m in range(17):
                mw = 64 if m == 16 else 128
                csl = slice(m * 128, m * 128 + mw)
                for tb in range(NTB):
                    sl = slice(tb * TB, (tb + 1) * TB)
                    for k in range(NF):
                        waits = [lnE, wnow]
                        if k == 0:
                            waits.append(ev)
                        p.on("tensor",
                             lambda e, m=m, k=k, tb=tb, sl=sl, csl=csl, mw=mw,
                             sl_=sl_: e.matmul(
                                 y_ps[tb % 4][0:mw, :], sl_[k][:, csl],
                                 xnE[k][:, sl],
                                 start=(k == 0), stop=(k == NF - 1)),
                             waits=waits, inc=(mm if k == NF - 1 else None))
                    p.on("scalar",
                         lambda e, tb=tb, m=m, mw=mw, h=h: e.activation(
                             out_bb[0:mw, :], y_ps[tb % 4][0:mw, :], AF.Identity,
                             bias=biasE[0:mw, h * 17 + m:h * 17 + m + 1]),
                         waits=(mm, st, tev, ev, ld), inc=ev)
                    if m < 8:
                        kw = (ev,) if h < 8 else (ev,) + ks0_done
                        p.dma("sync",
                              lambda h=h, m=m, sl=sl: ks_s[h // 8][
                                  (h % 8) * D + m * 128:
                                  (h % 8) * D + (m + 1) * 128, sl],
                              lambda: out_bb[:, :], st, waits=kw)
                    elif m < 16:
                        p.dma("sync",
                              lambda h=h, m=m, sl=sl: q2_l[
                                  h, (m - 8) * 128:(m - 7) * 128, sl],
                              lambda: out_bb[:, :], st, waits=(ev,))
                    else:
                        emit_twgroup(
                            p, C, lambda i: out_bb[:, i * 128:(i + 1) * 128],
                            t_ps, tt_sb,
                            lambda h=h, tb=tb: tok_view(vs_s, h * NTB + tb,
                                                        0, 64),
                            tmm, tev, st, waits=(ev,), wcols=64)
            if h == 7 and "ksbulk" not in SKIP:
                for hf in range(2):
                    p.dma_wreg("sync", KS[0],
                               [[SEQ, 4 * D], [2 * TB, NTB], [1, TB]],
                               hf * 4 * D * SEQ + q0(0, 0),
                               hf * 4 * D * SEQ + q0(0, 1),
                               lambda hf=hf: ks_s[0][
                                   hf * 4 * D:(hf + 1) * 4 * D, :].rearrange(
                                   "rw (tb c) -> rw tb c", c=TB),
                               st, waits=(st,))
                ks0_done = ((st.sem, st.v),)
        if "ksbulk" not in SKIP:
            for hf in range(2):
                p.dma_wreg("sync", KS[1],
                           [[SEQ, 4 * D], [2 * TB, NTB], [1, TB]],
                           hf * 4 * D * SEQ + q0(0, 0),
                           hf * 4 * D * SEQ + q0(0, 1),
                           lambda hf=hf: ks_s[1][
                               hf * 4 * D:(hf + 1) * 4 * D, :].rearrange(
                               "rw (tb c) -> rw tb c", c=TB),
                           st, waits=(st,))
        if "vsbulk" not in SKIP:
            p.dma_wreg("sync", VS,
                       [[SEQ * 64, 16], [2 * TB * 64, NTB], [64, TB], [1, 64]],
                       q0(0, 0) * 64, q0(0, 1) * 64,
                       lambda: vs_s[:, :].rearrange("(h tb r) d -> h tb r d",
                                                    tb=NTB, r=TB),
                       st, waits=(st,))
    phE()
    p.barrier()

    if upto < 4:
        return finish(p, nc, probes, locals())

    def phF():
        mskb = p.sbuf("mskF", [128, 8 * TB], BF16)
        p.on("gpsimd", lambda e: e.dma_start(mskb[:, :], msk_d[:, :]),
             inc=p.ctr("mld2"), inc_k=16)
        mld2 = p.ctr("mld2")
        sd3 = p.pair_sync(3)
        qt = [p.sbuf(f"qtF{dc}", [128, 2 * TB], BF16) for dc in range(NF)]
        kt_b = [p.sbuf(f"ktF{i}", [128, D], BF16) for i in range(2)]
        vt_b = [p.sbuf(f"vtF{i}", [128, 65], BF16) for i in range(2)]
        pt = [p.sbuf(f"ptF{i}", [128, TB], BF16) for i in range(2)]
        s_ps = [p.psum(f"spsF{i}", [128, TB], F32) for i in range(2)]
        o_ps = [p.psum(f"opsF{i}", [128, TB], F32) for i in range(2)]
        cbias = p.sbuf("cbF", [128, 1], F32)
        srow = p.sbuf("srF", [128, 1], F32)
        srow_b = p.sbuf("srbF", [128, 1], BF16)
        sc1 = p.sbuf("sc1F", [1, 1], F32)
        rrow = p.sbuf("rrF", [1, TB], F32)
        bc_sb = p.sbuf("bcF", [64, TB], F32)
        sa_bb = p.sbuf("sabF", [64, TB], BF16)
        vset = p.ctr("vset")
        p.on("vector", lambda e: e.memset(vt_b[0][:, 64:65], 1.0), inc=vset)
        p.on("vector", lambda e: e.memset(vt_b[1][:, 64:65], 1.0), inc=vset)

        qldF = p.ctr("qld")
        kldP = [p.ctr("kldA"), p.ctr("kldB")]
        mm = p.ctr("mm")
        ev = p.ctr("ev")

        def load_ktF(i, kt, h, extra=()):
            kldF = kldP[h % 2]
            p.on("scalar",
                 lambda e, i=i, kt=kt, h=h: e.dma_start(
                     kt_b[i][:, :].rearrange("p (dc j) -> p dc j", dc=NF),
                     KS[h // 8][h % 8, :, kt * 128:(kt + 1) * 128]
                     .rearrange("(dc p) j -> p dc j", p=128)),
                 waits=tuple(extra) + ((sd3, 1),), inc=kldF, inc_k=16)
            p.on("scalar",
                 lambda e, i=i, kt=kt, h=h: e.dma_start(
                     vt_b[i][:, 0:64], VS[h, kt * 128:(kt + 1) * 128, :]),
                 waits=tuple(extra) + ((sd3, 1), vset), inc=kldF, inc_k=16)

        def load_qF(qbg, h, extra=()):
            for dc in range(NF):
                for j in range(2):
                    tb_loc = 2 * qbg + j
                    p.dma("sync",
                          lambda dc=dc, j=j: qt[dc][:, j * TB:(j + 1) * TB],
                          lambda h=h, dc=dc, tb_loc=tb_loc: q2_l[
                              h, dc * 128:(dc + 1) * 128,
                              tb_loc * TB:(tb_loc + 1) * TB],
                          qldF, waits=extra)

        for h in range(16):
            for qbg in range(4):
                load_qF(qbg, h, extra=(mm, ev))
                if qbg == 0:
                    load_ktF(0, 0, h, extra=(ev,))
                    for dc in range(NF):
                        p.on("tensor",
                             lambda e, dc=dc: e.matmul(
                                 s_ps[0][:, :],
                                 kt_b[0][:, dc * 128:(dc + 1) * 128],
                                 qt[dc][:, 0:TB], start=(dc == 0),
                                 stop=(dc == NF - 1)),
                             waits=(kldP[h % 2], qldF),
                             inc=(mm if dc == NF - 1 else None))
                    p.on("vector", lambda e: e.reduce_max(srow[:, :],
                                                          s_ps[0][:, :], AX.X),
                         waits=(mm,), inc=ev)
                    p.on("vector", lambda e: e.tensor_copy(srow_b[:, :],
                                                           srow[:, :]),
                         waits=(ev,), inc=ev)
                    p.on("tensor",
                         lambda e: e.transpose(
                             o_ps[0][0:1, :].bitcast(BF16)[:, 0:128],
                             srow_b[:, :], C["ident"]),
                         waits=(ev,), inc=mm)
                    p.on("vector",
                         lambda e: e.reduce_max(
                             sc1[:, :], o_ps[0][0:1, :].bitcast(BF16)[:, 0:128],
                             AX.X),
                         waits=(mm,), inc=ev)
                    p.on("vector",
                         lambda e: e.tensor_scalar(sc1[:, :], sc1[:, :],
                                                   -1.0 / 32.0, -2.0, ALU.mult,
                                                   ALU.add),
                         waits=(ev,), inc=ev)
                    p.on("tensor",
                         lambda e: e.matmul(s_ps[1][:, 0:1], C["ones_row"],
                                            sc1[:, :], start=True, stop=True),
                         waits=(ev,), inc=mm)
                    p.on("vector", lambda e: e.tensor_copy(cbias[:, :],
                                                           s_ps[1][:, 0:1]),
                         waits=(mm,), inc=ev)
                nkt_g = 4 * (4 * qbg + 3 + 1)
                for kt in range(nkt_g):
                    load_ktF(kt % 2, kt, h, extra=((mm, ev) if kt >= 2
                                                   else (mm,)))
                    kld_now = (kldP[h % 2].sem, kldP[h % 2].v)
                    for j in range(2):
                        qq_lo = 4 * qbg + 0 + 2 * j
                        qq_hi = 4 * qbg + 1 + 2 * j
                        if kt >= 4 * (qq_hi + 1):
                            continue
                        for dc in range(NF):
                            waits = [kld_now, qldF]
                            if dc == 0:
                                waits.append(ev)
                            p.on("tensor",
                                 lambda e, kt=kt, j=j, dc=dc: e.matmul(
                                     s_ps[j][:, :],
                                     kt_b[kt % 2][:, dc * 128:(dc + 1) * 128],
                                     qt[dc][:, j * TB:(j + 1) * TB],
                                     start=(dc == 0), stop=(dc == NF - 1)),
                                 waits=waits, inc=(mm if dc == NF - 1 else None))
                        p.on("scalar",
                             lambda e, j=j: e.activation(
                                 pt[j][:, :], s_ps[j][:, :], AF.Exp,
                                 bias=cbias[:, :], scale=np.float32(1.0 / 32.0)),
                             waits=(mm,), inc=ev)
                        if kt >= 4 * qq_lo:
                            d = kt - 4 * qq_lo
                            p.on("vector",
                                 lambda e, j=j, d=d: e.tensor_tensor(
                                     pt[j][:, :], pt[j][:, :],
                                     mskb[:, d * TB:(d + 1) * TB],
                                     ALU.mult),
                                 waits=(ev, mld2), inc=ev)
                        p.on("tensor",
                             lambda e, kt=kt, j=j, qq_hi=qq_hi: e.matmul(
                                 o_ps[j][0:65, :], vt_b[kt % 2][:, :],
                                 pt[j][:, :], start=(kt == 0),
                                 stop=(kt == 4 * qq_hi + 3)),
                             waits=(ev,), inc=mm)
                for j in range(2):
                    tb_loc = 2 * qbg + j
                    p.on("vector",
                         lambda e, j=j: e.reciprocal(rrow[:, :],
                                                     o_ps[j][64:65, :]),
                         waits=(mm,), inc=ev)
                    p.on("tensor",
                         lambda e, j=j: e.matmul(
                             s_ps[j][0:64, :], C["ones_row"][:, 0:64],
                             rrow[:, :], start=True, stop=True),
                         waits=(ev,), inc=mm)
                    p.on("vector",
                         lambda e, j=j: e.tensor_copy(bc_sb[:, :],
                                                      s_ps[j][0:64, :]),
                         waits=(mm,), inc=ev)
                    p.on("vector",
                         lambda e, j=j: e.tensor_tensor(
                             sa_bb[:, :], o_ps[j][0:64, :], bc_sb[:, :],
                             ALU.mult),
                         waits=(ev, st), inc=ev)
                    p.dma("sync",
                          lambda h=h, tb_loc=tb_loc: sa_l[
                              h * 64:(h + 1) * 64,
                              tb_loc * TB:(tb_loc + 1) * TB],
                          lambda: sa_bb[:, :], st, waits=(ev,))
    phF()
    p.barrier()

    if upto < 5:
        return finish(p, nc, probes, locals())

    def emit_P(a_d, x_d, w_d, b_d, xo_d, s_d=None, sums_slot=None,
               sums_sd=None, final=False):
        wldP = p.ctr("wldP")
        pslab = emit_wslabs(p, w_d, NF, D, wldP, "pw")
        biasP = p.sbuf("biasP", [128, NF], F32)
        p.dma("scalar", lambda: biasP[:, :], lambda: b_d[:, :], ld)
        a_sb = [p.sbuf(f"aP{f}", [128, CF], BF16) for f in range(NF)]
        ald = p.ctr("ald")
        for f in range(NF):
            p.dma("sync", lambda f=f: a_sb[f][:, :],
                  lambda f=f: a_d[f * 128:(f + 1) * 128, :], ald)
        y_ps = [p.psum(f"yP{t}", [128, TB], F32) for t in range(4)]
        xr_sb = [p.sbuf(f"xrP{i}", [128, TB], F32) for i in range(2)]
        xo_sb = [p.sbuf(f"xoP{i}", [128, TB], F32) for i in range(2)]
        xldP = p.ctr("xld")
        mmP = p.ctr("mm")
        evP = p.ctr("ev")

        def evacP(m, tb):
            sl = slice(tb * TB, (tb + 1) * TB)
            w = (m * NTB + tb) % 2
            p.dma("sync", lambda w=w: xr_sb[w][:, :],
                  lambda m=m, sl=sl: x_d[m * 128:(m + 1) * 128, sl],
                  xldP, waits=(evP,))
            xnow = (xldP.sem, xldP.v)
            p.on("vector",
                 lambda e, m=m, w=w, tb=tb: e.scalar_tensor_tensor(
                     xo_sb[w][:, :], y_ps[tb % 4][:, :], biasP[:, m:m + 1],
                     xr_sb[w][:, :], ALU.add, ALU.add),
                 waits=(mmP, xnow, st), inc=evP)
            p.dma("sync", lambda m=m, sl=sl: xo_d[m * 128:(m + 1) * 128, sl],
                  lambda w=w: xo_sb[w][:, :], st, waits=(evP,))

        emit_mm_chain(
            p, lambda k, m: pslab[k][:, m * 128:(m + 1) * 128],
            lambda k, tb: a_sb[k][:, tb * TB:(tb + 1) * TB],
            y_ps, NF, NF, NTB, mmP, evP, [ald, ld], evacP, wld=wldP)

        if s_d is not None or sums_slot is not None:
            sums_sb = (p.sbuf("sumsP", [128, NF], F32)
                       if sums_slot is not None else None)
            lnP = p.ctr("ln2")
            ln_stream(p, C, xo_d, lnP, st, dst_d=s_d, sums_sb=sums_sb,
                      pre=(evP,))
            if sums_slot is not None:
                p.dma_wreg("sync", sums_slot, [[NF, 128], [1, NF]],
                           0, 128 * NF, lambda: sums_sb[:, :], st,
                           waits=(lnP,))

    emit_P(sa_l, x2_l, wp_d, bp_d, x3_l, s_d=s3_l, sums_slot=sums2)
    p.barrier()
    st = p.ctr("st3")

    if upto < 6:
        return finish(p, nc, probes, locals())

    emit_D(s3_l, x3_l, sums2, x4_l, 4)
    p.barrier()

    if upto < 7:
        return finish(p, nc, probes, locals())

    def phH():
        wldH = p.ctr("wldH")
        qsH = emit_wslabs(p, wq_d, NF, D, wldH, "wqH")
        ksH = emit_wslabs(p, wk_d, NF, D, wldH, "wkH")
        vsH = emit_wslabs(p, wv_d, NF, D, wldH, "wvH")
        biasH = p.sbuf("biasH", [128, 3 * NF], F32)
        p.dma("scalar", lambda: biasH[:, 0:NF], lambda: bq_d[:, :], ld)
        p.dma("scalar", lambda: biasH[:, NF:2 * NF], lambda: bk_d[:, :], ld)
        p.dma("scalar", lambda: biasH[:, 2 * NF:3 * NF], lambda: bv_d[:, :], ld)
        xnH = [p.sbuf(f"xnH{f}", [128, CF], BF16) for f in range(NF)]
        lnH = p.ctr("lnd")
        ln_stream(p, C, x4_l, lnH, st, dst_sb=xnH)
        slabsH = {0: qsH, 1: ksH, 2: vsH}
        y_ps = [p.psum(f"yH{t}", [128, TB], F32) for t in range(4)]
        t_ps = p.psum("tpsH", [128, 64], F32)
        out_bb = p.sbuf("obbH", [128, TB], BF16)
        tt_sb = p.sbuf("ttH", [128, TB], BF16)
        mm = p.ctr("mm")
        ev = p.ctr("ev")
        tmm = p.ctr("tmm")
        tev = p.ctr("tev")
        for w in range(3):
            for m in range(NF):
                for tb in range(NTB):
                    sl = slice(tb * TB, (tb + 1) * TB)
                    for k in range(NF):
                        waits = [lnH, (wldH.sem, 16 * NF * (w + 1))]
                        if k == 0:
                            waits.append(ev)
                        p.on("tensor",
                             lambda e, w=w, m=m, k=k, tb=tb, sl=sl: e.matmul(
                                 y_ps[tb % 4][:, :],
                                 slabsH[w][k][:, m * 128:(m + 1) * 128],
                                 xnH[k][:, sl],
                                 start=(k == 0), stop=(k == NF - 1)),
                             waits=waits, inc=(mm if k == NF - 1 else None))
                    p.on("scalar",
                         lambda e, w=w, m=m, tb=tb: e.activation(
                             out_bb[:, :], y_ps[tb % 4][:, :], AF.Identity,
                             bias=biasH[:, w * NF + m:w * NF + m + 1]),
                         waits=(mm, st, ev, tev), inc=ev)
                    if w == 0:
                        p.dma("sync",
                              lambda m=m, sl=sl: q1_l[m * 128:(m + 1) * 128, sl],
                              lambda: out_bb[:, :], st, waits=(ev,))
                    elif w == 1:
                        p.dma("sync",
                              lambda m=m, sl=sl: k1_s[m * 128:(m + 1) * 128, sl],
                              lambda: out_bb[:, :], st, waits=(ev,))
                    else:
                        emit_twgroup(
                            p, C, lambda i: out_bb[:, i * 128:(i + 1) * 128],
                            t_ps, tt_sb,
                            lambda m=m, tb=tb: tok_view(v1_s, tb, m * 128, 128),
                            tmm, tev, st, waits=(ev,))
        p.dma_wreg("sync", k1f, [[SEQ, D], [2 * TB, NTB], [1, TB]],
                   q0(0, 0), q0(0, 1),
                   lambda: k1_s[:, :].rearrange("rw (tb c) -> rw tb c", c=TB),
                   st, waits=(st,))
        p.dma_wreg("sync", v1t, [[2 * TB * D, NTB], [D, TB], [1, D]],
                   q0(0, 0) * D, q0(0, 1) * D,
                   lambda: v1_s[:, :].rearrange("(tb r) d -> tb r d", r=TB),
                   st, waits=(st,))
    phH()
    p.barrier()

    if upto < 8:
        return finish(p, nc, probes, locals())

    def phI():
        sd5 = p.pair_sync(5)
        q_sbI = [p.sbuf(f"qI{f}", [128, CF], BF16) for f in range(NF)]
        qldI = p.ctr("qld")
        for f in range(NF):
            p.dma("sync", lambda f=f: q_sbI[f][:, :],
                  lambda f=f: q1_l[f * 128:(f + 1) * 128, :], qldI)
        kft = p.sbuf("kftI", [128, SEQ], BF16)
        vt = p.sbuf("vtI", [128, NKT * 65], BF16)
        pt = [p.sbuf(f"ptI{i}", [128, TB], BF16) for i in range(4)]
        s_ps = [p.psum(f"spsI{i}", [128, TB], F32) for i in range(4)]
        o_ps = [p.psum(f"opsI{i}", [128, TB], F32) for i in range(4)]
        cbias = p.sbuf("cbI", [128, 1], F32)
        srow = p.sbuf("srI", [128, 1], F32)
        srow_b = p.sbuf("srbI", [128, 1], BF16)
        sc1 = p.sbuf("sc1I", [1, 1], F32)
        rrow = p.sbuf("rrI", [1, TB], F32)
        bc_sb = p.sbuf("bcI", [64, TB], F32)
        ao_sbI = [p.sbuf(f"aoI{f}", [128, CF], BF16) for f in range(NF)]
        vsetI = p.ctr("vset")
        p.on("vector", lambda e: e.memset(vt[:, :], 1.0), inc=vsetI)
        kldI = p.ctr("kld")
        mm = p.ctr("mm")
        ev = p.ctr("ev")
        st5 = (sd5, 1)
        for h in range(16):
            fo, po_ = h // 2, (h % 2) * 64
            if h % 2 == 0:
                p.on("sync",
                     lambda e, fo=fo: e.dma_start(
                         kft[:, :], k1f[fo * 128:(fo + 1) * 128, :]),
                     waits=(mm, ev, st5), inc=kldI, inc_k=16)
            p.on("sync",
                 lambda e, h=h: e.dma_start(
                     vt[:, :].rearrange("p (kt j) -> p kt j", j=65)[:, :, 0:64],
                     v1t[:, h * 64:(h + 1) * 64]
                     .rearrange("(kt p) j -> p kt j", p=128)),
                 waits=(mm, ev, st5, vsetI), inc=kldI, inc_k=16)
            kld_now = (kldI.sem, kldI.v)
            p.on("tensor",
                 lambda e, po_=po_, fo=fo: e.matmul(
                     s_ps[0][:, :], kft[po_:po_ + 64, 0:128],
                     q_sbI[fo][po_:po_ + 64, 0:TB], start=True, stop=True),
                 waits=(kld_now, qldI, ev), inc=mm)
            p.on("vector", lambda e: e.reduce_max(srow[:, :], s_ps[0][:, :],
                                                  AX.X),
                 waits=(mm,), inc=ev)
            p.on("vector", lambda e: e.tensor_copy(srow_b[:, :], srow[:, :]),
                 waits=(ev,), inc=ev)
            p.on("tensor",
                 lambda e: e.transpose(o_ps[0][0:1, :].bitcast(BF16)[:, 0:128],
                                       srow_b[:, :], C["ident"]),
                 waits=(ev,), inc=mm)
            p.on("vector",
                 lambda e: e.reduce_max(sc1[:, :],
                                        o_ps[0][0:1, :].bitcast(BF16)[:, 0:128],
                                        AX.X),
                 waits=(mm,), inc=ev)
            p.on("vector",
                 lambda e: e.tensor_scalar(sc1[:, :], sc1[:, :],
                                           -1.0 / 8.0, -2.0, ALU.mult, ALU.add),
                 waits=(ev,), inc=ev)
            p.on("tensor",
                 lambda e: e.matmul(s_ps[1][:, 0:1], C["ones_row"], sc1[:, :],
                                    start=True, stop=True),
                 waits=(ev,), inc=mm)
            p.on("vector", lambda e: e.tensor_copy(cbias[:, :], s_ps[1][:, 0:1]),
                 waits=(mm,), inc=ev)
            for qg in range(2):
                for kt in range(NKT):
                    for qbl in range(4):
                        qb = qg * 4 + qbl
                        p.on("tensor",
                             lambda e, kt=kt, qbl=qbl, qb=qb, fo=fo, po_=po_:
                             e.matmul(
                                 s_ps[qbl][:, :],
                                 kft[po_:po_ + 64, kt * 128:(kt + 1) * 128],
                                 q_sbI[fo][po_:po_ + 64,
                                           qb * TB:(qb + 1) * TB],
                                 start=True, stop=True),
                             waits=(kld_now, qldI, ev), inc=mm)
                        p.on("scalar",
                             lambda e, qbl=qbl: e.activation(
                                 pt[qbl][:, :], s_ps[qbl][:, :], AF.Exp,
                                 bias=cbias[:, :], scale=np.float32(1.0 / 8.0)),
                             waits=(mm,), inc=ev)
                        p.on("tensor",
                             lambda e, kt=kt, qbl=qbl: e.matmul(
                                 o_ps[qbl][0:65, :],
                                 vt[:, kt * 65:(kt + 1) * 65], pt[qbl][:, :],
                                 start=(kt == 0), stop=(kt == NKT - 1)),
                             waits=(ev,), inc=mm)
                for qbl in range(4):
                    qb = qg * 4 + qbl
                    p.on("vector",
                         lambda e, qbl=qbl: e.reciprocal(rrow[:, :],
                                                         o_ps[qbl][64:65, :]),
                         waits=(mm,), inc=ev)
                    p.on("tensor",
                         lambda e, qbl=qbl: e.matmul(
                             s_ps[qbl][0:64, :], C["ones_row"][:, 0:64],
                             rrow[:, :], start=True, stop=True),
                         waits=(ev,), inc=mm)
                    p.on("vector",
                         lambda e, qbl=qbl: e.tensor_copy(bc_sb[:, :],
                                                          s_ps[qbl][0:64, :]),
                         waits=(mm,), inc=ev)
                    p.on("vector",
                         lambda e, qbl=qbl, fo=fo, po_=po_, qb=qb:
                         e.tensor_tensor(
                             ao_sbI[fo][po_:po_ + 64, qb * TB:(qb + 1) * TB],
                             o_ps[qbl][0:64, :], bc_sb[:, :], ALU.mult),
                         waits=(ev,), inc=ev)
        stI = p.ctr("st")
        for f in range(NF):
            p.dma("sync", lambda f=f: ao1_l[f * 128:(f + 1) * 128, :],
                  lambda f=f: ao_sbI[f][:, :], stI, waits=(ev,))
    phI()
    p.barrier()

    if upto < 9:
        return finish(p, nc, probes, locals())

    emit_P(ao1_l, x4_l, wo_d, bo_d, x5_l, s_d=s5_l)
    p.barrier()

    if upto < 10:
        return finish(p, nc, probes, locals())

    def emit_R(s_d, x_d, xo_d, so_d, build_w):
        mmR = p.ctr("mm")
        evR = p.ctr("ev")
        s_sb = [p.sbuf(f"sR{f}", [128, CF], BF16) for f in range(NF)]
        sldR = p.ctr("ald")
        for f in range(NF):
            p.dma("gpsimd", lambda f=f: s_sb[f][:, :],
                  lambda f=f: s_d[f * 128:(f + 1) * 128, :], sldR)
        biasR = p.sbuf("biasR", [128, 8], F32)
        p.dma("scalar", lambda: biasR[:, 0:4], lambda: br_d[:, :], ld)
        p.dma("scalar", lambda: biasR[:, 4:8], lambda: bi_d[:, :], ld)
        subR = ExitStack()
        oldR = p.phase_stack
        if build_w:
            p.phase_stack = subR
            g2row = p.sbuf("g2row", [1, 128 * 112], BF16)
            g1row = p.sbuf("g1row", [1, 256 * 32], BF16)
            p.on("gpsimd", lambda e: e.dma_start(
                g2row[:, :].rearrange("o (r c) -> o r c", c=112),
                g2_d[:, :].rearrange("(o r) c -> o r c", o=1)),
                inc=ld, inc_k=16)
            p.on("gpsimd", lambda e: e.dma_start(
                g1row[:, :].rearrange("o (r c) -> o r c", c=32),
                g1_d[:, :].rearrange("(o r) c -> o r c", o=1)),
                inc=ld, inc_k=16)
            ldnow = (ld.sem, ld.v)
            w_sb = [[p.sbuf(f"wR{ri}_{t}", [128, 512], BF16)
                     for t in range(4)] for ri in range(2)]
            w_ps = p.psum("wpsR", [128, TB], F32)
            t_ps2 = p.psum("tpsR", [128, 128], BF16)
            for ri in range(2):
                for t in range(4):
                    for wnd in range(2):
                        for g in range(16):
                            for c in range(4):
                                for par in range(4):
                                    b_ = 8 * t + 4 * wnd + par
                                    li = (ri * 4 + c) * 16 + g
                                    off = li * 112 + 48 - par * 16
                                    gi = (ri * 4 + c) * 32 + b_
                                    last = (wnd == 1 and g == 15 and c == 3
                                            and par == 3)
                                    p.on("tensor",
                                         lambda e, wnd=wnd, g=g, off=off,
                                         gi=gi,
                                         first=(c == 0 and par == 0),
                                         fin=(c == 3 and par == 3):
                                         e.matmul(
                                             w_ps[64 * wnd:64 * wnd + 64,
                                                  g * 32:(g + 1) * 32],
                                             g2row[:, off:off + 64],
                                             g1row[:, gi * 32:gi * 32 + 32],
                                             start=first, stop=fin,
                                             skip_group_check=True),
                                         waits=([ldnow, evR]
                                                if (g == 0 and c == 0
                                                    and par == 0)
                                                else [ldnow]),
                                         inc=(mmR if last else None))
                    for g in range(16):
                        p.on("vector",
                             lambda e, ri=ri, t=t, g=g: e.tensor_copy(
                                 w_sb[ri][t][:, :].rearrange(
                                     "p (d g2) -> p g2 d", g2=16)[:, g, :],
                                 w_ps[:, g * 32:(g + 1) * 32]),
                             waits=(mmR,), inc=evR)
            for ri in range(2):
                for t in range(4):
                    for u in range(4):
                        p.on("tensor",
                             lambda e, ri=ri, t=t, u=u: e.transpose(
                                 t_ps2[:, :],
                                 w_sb[ri][t][:, u * 128:(u + 1) * 128],
                                 C["ident"]),
                             waits=(evR,), inc=mmR)
                        p.on("vector",
                             lambda e, ri=ri, t=t, u=u: e.tensor_copy(
                                 wt_sb[ri][u][:, t * 128:(t + 1) * 128],
                                 t_ps2[:, :]),
                             waits=(mmR,), inc=evR)
            for u in range(4):
                p.on("vector",
                     lambda e, u=u: e.tensor_scalar_mul(wtn_sb[u][:, :],
                                                        wt_sb[1][u][:, :],
                                                        -1.0),
                     waits=(evR,), inc=evR)
        p.phase_stack = oldR
        subR.close()
        wdone = (evR.sem, evR.v)

        y_ps = [p.psum(f"yR{t}", [128, TB], F32) for t in range(2)]
        xr_sb = [p.sbuf(f"xrR{i}", [128, TB], F32) for i in range(2)]
        xo_sb = [p.sbuf(f"xoR{i}", [128, TB], F32) for i in range(2)]
        xldR = p.ctr("xld")
        mm2 = p.ctr("mm2")
        ev2 = p.ctr("ev2")
        for half in range(2):
            for m in range(4):
                for tb in range(NTB):
                    sl = slice(tb * TB, (tb + 1) * TB)
                    for k in range(8):
                        if half == 0:
                            wmat = (wt_sb[0][k][:, m * 128:(m + 1) * 128]
                                    if k < 4 else
                                    wtn_sb[k - 4][:, m * 128:(m + 1) * 128])
                        else:
                            wmat = (wt_sb[1][k][:, m * 128:(m + 1) * 128]
                                    if k < 4 else
                                    wt_sb[0][k - 4][:, m * 128:(m + 1) * 128])
                        waits = [sldR, wdone]
                        if k == 0:
                            waits.append(ev2)
                        p.on("tensor",
                             lambda e, wmat=wmat, k=k, tb=tb, sl=sl:
                             e.matmul(y_ps[tb % 2][:, :], wmat,
                                      s_sb[k][:, sl],
                                      start=(k == 0), stop=(k == 7)),
                             waits=waits, inc=(mm2 if k == 7 else None))
                    fo = half * 4 + m
                    w = (half * 32 + m * NTB + tb) % 2
                    p.dma("sync", lambda w=w: xr_sb[w][:, :],
                          lambda fo=fo, sl=sl: x_d[fo * 128:(fo + 1) * 128,
                                                   sl],
                          xldR, waits=(ev2, wdone))
                    xnow = (xldR.sem, xldR.v)
                    p.on("vector",
                         lambda e, fo=fo, w=w, tb=tb, half=half, m=m:
                         e.scalar_tensor_tensor(
                             xo_sb[w][:, :], y_ps[tb % 2][:, :],
                             biasR[:, half * 4 + m:half * 4 + m + 1],
                             xr_sb[w][:, :], ALU.add, ALU.add),
                         waits=(mm2, xnow, st), inc=ev2)
                    p.dma("sync",
                          lambda fo=fo, sl=sl: xo_d[fo * 128:(fo + 1) * 128,
                                                    sl],
                          lambda w=w: xo_sb[w][:, :], st, waits=(ev2,))
        if so_d is not None:
            lnR = p.ctr("ln2")
            ln_stream(p, C, xo_d, lnR, st, dst_d=so_d, pre=(p.ctr("ev2"),))

    emit_R(s5_l, x5_l, x6_l, s6_l, build_w=True)
    p.barrier()

    if upto < 11:
        return finish(p, nc, probes, locals())

    def phK1():
        wldK = p.ctr("wldK")
        wslabK = emit_wslabs(p, wqkv1_d, NF, 3 * D, wldK, "wK")
        s_sbK = [p.sbuf(f"sK{f}", [128, CF], BF16) for f in range(NF)]
        sldK = p.ctr("ald")
        for f in range(NF):
            p.dma("gpsimd", lambda f=f: s_sbK[f][:, :],
                  lambda f=f: s6_l[f * 128:(f + 1) * 128, :], sldK)
        y_ps = [p.psum(f"yK{t}", [128, TB], F32) for t in range(4)]
        t_ps = p.psum("tpsK", [128, 64], F32)
        t1 = p.sbuf("t1K", [128, TB], F32)
        t2 = p.sbuf("t2K", [128, TB], F32)
        out_bb = p.sbuf("obbK", [128, TB], BF16)
        tt_sb = p.sbuf("ttK", [128, TB], BF16)
        mm = p.ctr("mm")
        ev = p.ctr("ev")
        tmm = p.ctr("tmm")
        tev = p.ctr("tev")
        for m in range(24):
            s_part = m // NF
            fo = m % NF
            for tb in range(NTB):
                sl = slice(tb * TB, (tb + 1) * TB)
                for k in range(NF):
                    waits = [sldK, (wldK.sem, 16 * (k + 1))]
                    if k == 0:
                        waits.append(ev)
                    p.on("tensor",
                         lambda e, m=m, k=k, tb=tb, sl=sl: e.matmul(
                             y_ps[tb % 4][:, :],
                             wslabK[k][:, m * 128:(m + 1) * 128],
                             s_sbK[k][:, sl], start=(k == 0),
                             stop=(k == NF - 1)),
                         waits=waits, inc=(mm if k == NF - 1 else None))
                if s_part == 2:
                    p.on("scalar",
                         lambda e, tb=tb: e.activation(out_bb[:, :],
                                                       y_ps[tb % 4][:, :],
                                                       AF.Identity),
                         waits=(mm, st, tev, ev), inc=ev)
                else:
                    p.on("vector",
                         lambda e, tb=tb: e.tensor_scalar_min(
                             t1[:, :], y_ps[tb % 4][:, :], 0.0),
                         waits=(mm, ev, st, tev), inc=ev)
                    p.on("scalar",
                         lambda e: e.activation(t2[:, :], t1[:, :], AF.Exp),
                         waits=(ev,), inc=ev)
                    p.on("vector",
                         lambda e, tb=tb: e.tensor_scalar_max(
                             t1[:, :], y_ps[tb % 4][:, :], 0.0),
                         waits=(ev,), inc=ev)
                    p.on("vector",
                         lambda e: e.tensor_tensor(out_bb[:, :], t1[:, :],
                                                   t2[:, :], ALU.add),
                         waits=(ev,), inc=ev)
                if s_part == 0:
                    p.dma("sync",
                          lambda fo=fo, sl=sl: phq_l[fo * 128:(fo + 1) * 128, sl],
                          lambda: out_bb[:, :], st, waits=(ev,))
                else:
                    dst = pk_s if s_part == 1 else v2_s
                    emit_twgroup(
                        p, C, lambda i: out_bb[:, i * 128:(i + 1) * 128],
                        t_ps, tt_sb,
                        lambda dst=dst, fo=fo, tb=tb: tok_view(dst, tb,
                                                               fo * 128, 128),
                        tmm, tev, st, waits=(ev,))
        for stage, shd in ((pk_s, pkt), (v2_s, v2t)):
            p.dma_wreg("sync", shd, [[2 * TB * D, NTB], [D, TB], [1, D]],
                       q0(0, 0) * D, q0(0, 1) * D,
                       lambda stage=stage: stage[:, :].rearrange(
                           "(tb r) d -> tb r d", r=TB),
                       st, waits=(st,))
    phK1()
    p.barrier()

    if upto < 12:
        return finish(p, nc, probes, locals())

    def phL1b():
        sd6 = p.pair_sync(6)
        q_sbL = [p.sbuf(f"qL{f}", [128, CF], BF16) for f in range(NF)]
        qldL = p.ctr("qld")
        for f in range(NF):
            p.dma("sync", lambda f=f: q_sbL[f][:, :],
                  lambda f=f: phq_l[f * 128:(f + 1) * 128, :], qldL)
        ka_t = [p.sbuf(f"kaL{i}", [128, 65], BF16) for i in range(2)]
        vk_t = [p.sbuf(f"vkL{i}", [128, 128], BF16) for i in range(2)]
        kvz_ps = p.psum("kvzL", [65, 128], F32)
        z_ps = p.psum("zL", [64, 1], F32)
        kv_bf = p.sbuf("kvbL", [128, 64], BF16)
        z_bf = p.sbuf("zbL", [128, 1], BF16)
        n_ps = p.psum("nL", [64, TB], F32)
        d_ps = p.psum("dL", [32, TB], F32)
        b_ps = p.psum("bL", [64, TB], F32)
        drow = p.sbuf("drL", [1, TB], F32)
        bc_sb = p.sbuf("bcL", [64, TB], F32)
        ao_sbL = [p.sbuf(f"aoL{f}", [128, CF], BF16) for f in range(NF)]
        vsetL = p.ctr("vset")
        p.on("vector", lambda e: e.memset(ka_t[0][:, 64:65], 1.0), inc=vsetL)
        p.on("vector", lambda e: e.memset(ka_t[1][:, 64:65], 1.0), inc=vsetL)
        kldL = p.ctr("kld")
        mm = p.ctr("mm")
        ev = p.ctr("ev")
        stL = (sd6, 1)
        for h in range(16):
            fo, po_ = h // 2, (h % 2) * 64
            marks = []
            for tt in range(NKT):
                waits = [stL, vsetL] if tt < 2 else [stL, (mm.sem,
                                                          marks[tt - 2])]
                p.on("scalar",
                     lambda e, h=h, tt=tt: e.dma_start(
                         ka_t[tt % 2][:, 0:64],
                         pkt[tt * 128:(tt + 1) * 128, h * 64:(h + 1) * 64]),
                     waits=waits, inc=kldL, inc_k=16)
                p.on("scalar",
                     lambda e, h=h, tt=tt: e.dma_start(
                         vk_t[tt % 2][:, 0:64],
                         v2t[tt * 128:(tt + 1) * 128, h * 64:(h + 1) * 64]),
                     waits=waits, inc=kldL, inc_k=16)
                p.on("scalar",
                     lambda e, h=h, tt=tt: e.dma_start(
                         vk_t[tt % 2][:, 64:128],
                         pkt[tt * 128:(tt + 1) * 128, h * 64:(h + 1) * 64]),
                     waits=waits, inc=kldL, inc_k=16)
                p.on("tensor",
                     lambda e, tt=tt: e.matmul(
                         kvz_ps[:, :], ka_t[tt % 2][:, :], vk_t[tt % 2][:, :],
                         start=(tt == 0), stop=(tt == NKT - 1)),
                     waits=[(kldL.sem, kldL.v), ev], inc=mm)
                p.on("tensor",
                     lambda e, tt=tt: e.matmul(
                         z_ps[:, :], ka_t[tt % 2][:, 0:64], C["ones_col_bf"],
                         start=(tt == 0), stop=(tt == NKT - 1)),
                     waits=[(kldL.sem, kldL.v), ev], inc=mm)
                marks.append(mm.v)
            p.on("vector",
                 lambda e: e.tensor_copy(kv_bf[0:64, :], kvz_ps[0:64, 0:64]),
                 waits=(mm,), inc=ev)
            p.on("vector",
                 lambda e: e.tensor_copy(kv_bf[64:128, :], kvz_ps[0:64, 0:64]),
                 waits=(mm,), inc=ev)
            p.on("vector", lambda e: e.tensor_copy(z_bf[0:64, :], z_ps[:, :]),
                 waits=(mm,), inc=ev)
            p.on("vector", lambda e: e.tensor_copy(z_bf[64:128, :], z_ps[:, :]),
                 waits=(mm,), inc=ev)
            for qbl in range(NTB):
                sl = slice(qbl * TB, (qbl + 1) * TB)
                p.on("tensor",
                     lambda e, fo=fo, po_=po_, sl=sl: e.matmul(
                         n_ps[:, :], kv_bf[po_:po_ + 64, :],
                         q_sbL[fo][po_:po_ + 64, sl], start=True, stop=True),
                     waits=(ev, qldL), inc=mm)
                p.on("tensor",
                     lambda e, fo=fo, po_=po_, sl=sl: e.matmul(
                         d_ps[0:1, :], z_bf[po_:po_ + 64, :],
                         q_sbL[fo][po_:po_ + 64, sl], start=True, stop=True),
                     waits=(ev, qldL), inc=mm)
                p.on("vector",
                     lambda e: e.tensor_scalar_add(drow[:, :], d_ps[0:1, :],
                                                   1e-6),
                     waits=(mm,), inc=ev)
                p.on("vector", lambda e: e.reciprocal(drow[:, :], drow[:, :]),
                     waits=(ev,), inc=ev)
                p.on("tensor",
                     lambda e: e.matmul(b_ps[:, :], C["ones_row"][:, 0:64],
                                        drow[:, :], start=True, stop=True),
                     waits=(ev,), inc=mm)
                p.on("vector", lambda e: e.tensor_copy(bc_sb[:, :], b_ps[:, :]),
                     waits=(mm,), inc=ev)
                p.on("vector",
                     lambda e, fo=fo, po_=po_, sl=sl: e.tensor_tensor(
                         ao_sbL[fo][po_:po_ + 64, sl], n_ps[:, :], bc_sb[:, :],
                         ALU.mult),
                     waits=(ev,), inc=ev)
        stL2 = p.ctr("st")
        for f in range(NF):
            p.dma("sync", lambda f=f: ao2_l[f * 128:(f + 1) * 128, :],
                  lambda f=f: ao_sbL[f][:, :], stL2, waits=(ev,))
    phL1b()
    p.barrier()

    if upto < 13:
        return finish(p, nc, probes, locals())

    emit_P(ao2_l, x6_l, wout_d, bout_d, x7_l, s_d=s7_l)
    p.barrier()

    if upto < 14:
        return finish(p, nc, probes, locals())

    emit_R(s7_l, x7_l, xB, None, build_w=False)
    p.barrier()
    fst = p.ctr("fst")
    p.dma("sync", lambda: out_d[:, :], lambda: xB[:, :], fst)
    p.barrier()

    return finish(p, nc, probes, locals())


_last_p = [None]


def finish(p, nc, probes, env):
    _last_p[0] = p
    outs = []
    pst = p.ctr("pst")
    for name in probes:
        t = env[name]
        shape = list(t.shape)
        od = nc.dram_tensor(f"probe_{name}", shape,
                            I32 if t.dtype == I32 else F32,
                            kind="ExternalOutput")
        outs.append(name)
        if t.dtype == BF16:
            p.on("gpsimd", lambda e, od=od, t=t: e.dma_start(
                od[:, :] if len(shape) == 2 else od[:, :, :],
                t[:, :] if len(shape) == 2 else t[:, :, :]),
                inc=pst, inc_k=16)
        else:
            p.dma("sync",
                  lambda od=od: od[:, :] if len(shape) == 2 else od[:, :, :],
                  lambda t=t: t[:, :] if len(shape) == 2 else t[:, :, :],
                  pst)
    if probes:
        p.barrier()
    return p.emit(), p.m0, p.m1



def mycols(role):
    return np.concatenate([np.arange(q0(t, role), q0(t, role) + TB)
                           for t in range(NTB)])


def _cols(b, n):
    return np.ascontiguousarray(np.asarray(b, np.float32).reshape(n, 128).T)


def host_inputs(inputs):
    g = lambda n: np.asarray(inputs[n], np.float32)
    consts = host_consts()
    bkqv = g("L0_attn_b_bkqv")
    bcols = np.zeros((16, 128, 17), np.float32)
    for h in range(16):
        bcols[h, :, 0:16] = bkqv[h, 0:2048].reshape(16, 128).T
        bcols[h, 0:64, 16] = bkqv[h, 2048:2112]
    G1 = [g("L1_ffn_G1_R"), g("L1_ffn_G1_I")]
    G2 = [g("L1_ffn_G2_R"), g("L1_ffn_G2_I")]
    g2lhs = np.zeros((128, 112), np.float32)
    g1rhs = np.zeros((256, 32), np.float32)
    for ri in range(2):
        for c in range(4):
            for gg in range(16):
                g2lhs[(ri * 4 + c) * 16 + gg, 48:64] = G2[ri][c, :, 0, gg]
            for b_ in range(32):
                g1rhs[(ri * 4 + c) * 32 + b_, :] = G1[ri][0, b_, c, :]
    w = {
        "qkv_w": g("L0_attn_a_qkv_w"),
        "qkv_b_cols": _cols(g("L0_attn_a_qkv_b"), 24),
        "proj_w": g("L0_attn_a_proj_w"),
        "proj_b_cols": _cols(g("L0_attn_a_proj_b"), 8),
        "Wkqv": g("L0_attn_b_Wkqv"), "bkqv_cols": bcols,
        "Wp": g("L0_attn_b_Wp"), "bp_cols": _cols(g("L0_attn_b_bp"), 8),
        "cn_w1": g("L0_ffn_cn_w1"), "cn_w2": g("L0_ffn_cn_w2"),
        "cn_w3": g("L0_ffn_cn_w3"), "cn_w4": g("L0_ffn_cn_w4"),
        "cn_b1_cols": _cols(g("L0_ffn_cn_b1"), 8),
        "cn_b2_cols": _cols(g("L0_ffn_cn_b2"), 4),
        "cn_b3_cols": _cols(g("L0_ffn_cn_b3"), 2),
        "cn_b4": g("L0_ffn_cn_b4").reshape(1, 1),
        "gc_w1": g("L0_ffn_gc_w1"), "gc_w2": g("L0_ffn_gc_w2"),
        "gc_w3": g("L0_ffn_gc_w3"),
        "gc_b1_cols": _cols(g("L0_ffn_gc_b1"), 4),
        "gc_b2_cols": _cols(g("L0_ffn_gc_b2"), 2),
        "gc_b3": g("L0_ffn_gc_b3").reshape(1, 1),
        "ar": g("L0_ffn_adaptive_range").reshape(1, 1),
        "Wq": g("L1_attn_a_Wq"), "Wk": g("L1_attn_a_Wk"),
        "Wv": g("L1_attn_a_Wv"),
        "bq_cols": _cols(g("L1_attn_a_bq"), 8),
        "bk_cols": _cols(g("L1_attn_a_bk"), 8),
        "bv_cols": _cols(g("L1_attn_a_bv"), 8),
        "Wo": g("L1_attn_a_Wo"), "bo_cols": _cols(g("L1_attn_a_bo"), 8),
        "g2lhs": g2lhs, "g1rhs": g1rhs,
        "br_cols": _cols(g("L1_ffn_bias_real"), 4),
        "bi_cols": _cols(g("L1_ffn_bias_imag"), 4),
        "w_qkv": g("L1_attn_b_w_qkv"),
        "w_out": g("L1_attn_b_w_out"),
        "bout_cols": _cols(g("L1_attn_b_b_out"), 8),
        **consts,
    }
    return w


def role_mask(role):
    m = np.zeros((128, 8 * TB), np.float32)
    kk, jj = np.meshgrid(np.arange(128), np.arange(TB), indexing="ij")
    for d in range(8):
        dd = d - 4 * role
        if dd < 0:
            m[:, d * TB:(d + 1) * TB] = 1.0
        elif dd <= 3:
            m[:, d * TB:(d + 1) * TB] = (kk <= jj - 128 * dd)
    return m


def make_in_maps(inputs, m0, m1):
    import random
    w = host_inputs(inputs)
    x = np.asarray(inputs["L0_attn_a_x"], np.float32)
    sess = random.randrange(1, 2 ** 30)
    metas = []
    for mm_ in (m0, m1):
        a = np.zeros((1, MMAX), np.int32)
        a[0, :len(mm_)] = np.asarray(mm_, np.int64).astype(np.int32)
        a[0, 4] = sess
        a[0, 5] = sess ^ 0x2AAAAAAA
        metas.append(a)
    masks = [role_mask(0), role_mask(1)]
    in_maps = []
    for c in range(NCORE):
        b = c // 4
        role = c % 2
        xT = x[b].T
        x_my = np.ascontiguousarray(xT[:, mycols(role)])
        in_maps.append({"x_my": x_my, "meta": metas[role],
                        "mask": masks[role], **w})
    return in_maps


def kernel(**inputs):
    nc, m0, m1 = build_fused()
    in_maps = make_in_maps(inputs, m0, m1)
    res = run_bass_kernel_spmd(nc, in_maps, core_ids=list(range(NCORE)))
    out = np.zeros((B, D, SEQ), np.float32)
    for b in range(B):
        for role in range(2):
            c = b * 4 + role
            out[b][:, mycols(role)] = np.asarray(res.results[c]["x8_o"],
                                                 np.float32)
    return np.ascontiguousarray(out.transpose(0, 2, 1))
